# version 19
# baseline (speedup 1.0000x reference)
import math
from contextlib import ExitStack

import numpy as np
import ml_dtypes
import concourse.bass as bass
import concourse.mybir as mybir
from concourse.bass_utils import run_bass_kernel_spmd

F32 = mybir.dt.float32
BF16 = mybir.dt.bfloat16
AF = mybir.ActivationFunctionType
ALU = mybir.AluOpType
AX = mybir.AxisListType

D = 1024
T = 2048
NC = 8
H = 16
HD = 64
FF = 4096
FC = FF // 128
NTT = 4
BF = ml_dtypes.bfloat16

EPS = 1e-5


def _slopes16():
    s = 2.0 ** (-0.5)
    return [s ** (i + 1) for i in range(16)]



ENGS = ("sync", "scalar", "vector", "tensor", "gpsimd")


class Em:
    def __init__(self, nc, es):
        self.nc = nc
        self.q = []
        self.sems = {}
        self.dmasems = {
            "sync": [f"dqs{i}" for i in range(4)],
            "gpsimd": [f"dqg{i}" for i in range(8)],
            "scalar": ["dqc0"],
        }
        names = list(ENGS)
        for lst in self.dmasems.values():
            names += lst
        for name in names:
            self.sems[name] = es.enter_context(nc.semaphore("S_" + name))
        self.cnt = {k: 0 for k in self.sems}
        self.dmaidx = {k: 0 for k in self.dmasems}
        self.last_wait = {}

    def _push(self, eng, fn, after, sem_name, amt):
        waits = []
        seen = {}
        for tok in after:
            if tok is None:
                continue
            s, v = tok
            if v > seen.get(s, -1):
                seen[s] = v
        for s, v in seen.items():
            k = (eng, s)
            if self.last_wait.get(k, -1) >= v:
                continue
            self.last_wait[k] = v
            waits.append((self.sems[s], v))
        inc = None
        tok = None
        if sem_name is not None:
            self.cnt[sem_name] += amt
            tok = (sem_name, self.cnt[sem_name])
            inc = (self.sems[sem_name], amt)
        self.q.append((eng, waits, fn, inc))
        return tok

    def op(self, eng, fn, after=()):
        return self._push(eng, fn, after, eng, 1)

    def dma(self, eng, fn, after=()):
        lst = self.dmasems[eng]
        i = self.dmaidx[eng] % len(lst)
        self.dmaidx[eng] += 1
        sem = lst[i]
        after = list(after)
        if self.cnt[sem] > 0:
            after.append((sem, self.cnt[sem]))
        return self._push(eng, fn, after, sem, 16)

    def barrier(self):
        toks = [(s, self.cnt[s]) for s in self.cnt if self.cnt[s] > 0]
        for eng in ENGS:
            self._push(eng, None, toks, None, 0)

    def play(self, block):
        for eng_name in ENGS:
            dec = getattr(block, eng_name)

            def body(e, _eng=eng_name):
                for (eng, waits, fn, inc) in self.q:
                    if eng != _eng:
                        continue
                    for (sem, v) in waits:
                        e.wait_ge(sem, v)
                    if fn is None:
                        continue
                    instr = fn(e)
                    if inc is not None:
                        instr.then_inc(*inc)

            dec(body)


class Buf:

    def __init__(self, ap):
        self.ap = ap
        self.w = None
        self.r = []

    def wdeps(self):
        return ([self.w] if self.w else []) + self.r

    def rdeps(self):
        return [self.w] if self.w else []

    def wrote(self, tok):
        self.w = tok
        self.r = []

    def read(self, tok):
        self.r.append(tok)



class Build:

    def __init__(self, blocks):
        self.blocks = blocks
        nc = bass.Bass("TRN2", target_bir_lowering=False)
        self.nc = nc
        self.es = ExitStack()
        self.em = Em(nc, self.es)
        self.din = {}
        self.need = set()

    def dram_in(self, name, shape, dtype=F32):
        ap = self.nc.dram_tensor(name, shape, dtype, kind="ExternalInput").ap()
        self.din[name] = ap
        self.need.add(name)
        return ap

    def sb(self, name, shape, dtype=F32, es=None):
        return Buf((es or self.es).enter_context(
            self.nc.sbuf_tensor(name, shape, dtype)))

    def ps(self, name, es=None):
        return Buf((es or self.es).enter_context(
            self.nc.psum_tensor(name, [128, 512], F32)))


    def setup_consts(self):
        em, nc = self.em, self.nc
        self.CP = self.sb("CP", [128, 160])
        cp = self.dram_in("cpack", [128, 160])
        t = em.dma("sync", lambda e: e.dma_start(out=self.CP.ap[:], in_=cp))
        self.CP.wrote(t)
        self.OB1 = self.sb("OB1", [1, 128], BF16)
        self.OB1.wrote(em.op("vector", lambda e: e.memset(self.OB1.ap[:], 1.0)))
        self.OC = self.sb("OC", [128, 1], BF16)
        self.OC.wrote(em.op("vector", lambda e: e.memset(self.OC.ap[:], 1.0)))
        self.EPS1 = self.sb("EPS1", [1, 1], F32)
        self.EPS1.wrote(em.op("vector", lambda e: e.memset(self.EPS1.ap[:], EPS)))
        self.BIS = self.sb("BIS", [1, 256], F32)
        bisd = self.dram_in("bisrow", [1, 256])
        self.BIS.wrote(em.dma("sync", lambda e: e.dma_start(out=self.BIS.ap[:], in_=bisd)))
        self.P = [self.ps(f"P{i}") for i in range(8)]

    def load_stream(self, xname):
        em = self.em
        self.XR = [self.sb(f"XR{kc}", [128, T]) for kc in range(NC)]
        self.XB = [self.sb(f"XB{kc}", [128, T], BF16) for kc in range(NC)]
        xT = self.dram_in(xname, [D, T])
        for kc in range(NC):
            src = xT.rearrange("(c p) t -> c p t", p=128)[kc]
            self.XR[kc].wrote(em.dma(
                "sync", lambda e, s=src, d=self.XR[kc]: e.dma_start(out=d.ap[:], in_=s)))
            self.XB[kc].wrote(em.dma(
                "gpsimd", lambda e, s=src, d=self.XB[kc]: e.dma_start(out=d.ap[:], in_=s)))

    def store_stream(self, yname):
        em = self.em
        yT = self.nc.dram_tensor(yname, [D, T], F32, kind="ExternalOutput").ap()
        toks = []
        for kc in range(NC):
            dst = yT.rearrange("(c p) t -> c p t", p=128)[kc]
            toks.append(em.dma(
                "sync" if kc % 2 == 0 else "scalar",
                lambda e, d=dst, s=self.XR[kc]: e.dma_start(out=d, in_=s.ap[:]),
                after=self.XR[kc].rdeps()))
        self.em._push("sync", None, toks, None, 0)

    def cast_weight(self, wname, shape, sbbuf, after=()):
        em = self.em
        src = self.din[wname] if wname in self.din else self.dram_in(wname, shape)
        t = em.dma("gpsimd",
                   lambda e: e.dma_start(out=sbbuf.ap[:], in_=src),
                   after=list(after) + sbbuf.wdeps())
        sbbuf.wrote(t)
        return t


    def layernorm(self, tag, last=False):
        em = self.em
        es = ExitStack()
        SQ = [self.sb(f"{tag}sq{i}", [128, 512], BF16, es) for i in range(4)]
        SR2 = [self.sb(f"{tag}sr{i}", [1, 512 * 6], F32, es) for i in range(2)]
        ABS2 = [self.sb(f"{tag}abs{i}", [128, 1024], F32, es) for i in range(2)]
        AB2 = [self.sb(f"{tag}ab{i}", [1, 512 * 2], BF16, es) for i in range(2)]
        TMP = [self.sb(f"{tag}tmp{i}", [128, 512], F32, es) for i in range(4)]
        PsumL = [self.P[6], self.P[4]]
        PsqL = [self.P[7], self.P[5]]
        PaL = [self.P[0], self.P[2]]
        PbL = [self.P[1], self.P[3]]
        for kc in range(NC):
            tcb = em.op("vector", lambda e, k=kc: e.tensor_copy(
                self.XB[k].ap[:], self.XR[k].ap[:]),
                after=self.XR[kc].rdeps() + self.XB[kc].wdeps())
            self.XB[kc].wrote(tcb)
            self.XR[kc].read(tcb)
        for tt in range(NTT):
            col = tt * 512
            Psum, Psq = PsumL[tt % 2], PsqL[tt % 2]
            Pa, Pb = PaL[tt % 2], PbL[tt % 2]
            for kc in range(NC):
                sq = SQ[(kc % 2) + 2 * (tt % 2)]
                tsq = em.op("scalar", lambda e, s=sq, k=kc, c=col: e.activation(
                    s.ap[:], self.XR[k].ap[:, c:c + 512], AF.Square),
                    after=self.XR[kc].rdeps() + sq.wdeps())
                sq.wrote(tsq)
                self.XR[kc].read(tsq)
                tm = em.op("tensor", lambda e, k=kc, c=col: e.matmul(
                    Psum.ap[0:1, :], self.OC.ap[:], self.XB[k].ap[:, c:c + 512],
                    start=(k == 0), stop=(k == NC - 1)),
                    after=self.XB[kc].rdeps() + self.OC.rdeps() +
                          (Psum.wdeps() if kc == 0 else []))
                self.XB[kc].read(tm)
                if kc == NC - 1:
                    Psum.wrote(tm)
                tq = em.op("tensor", lambda e, s=sq, k=kc: e.matmul(
                    Psq.ap[0:1, :], self.OC.ap[:], s.ap[:],
                    start=(k == 0), stop=(k == NC - 1)),
                    after=sq.rdeps() + (Psq.wdeps() if kc == 0 else []))
                sq.read(tq)
                if kc == NC - 1:
                    Psq.wrote(tq)
            SR = SR2[tt % 2]
            AB = AB2[tt % 2]
            ABS_ = ABS2[tt % 2]
            m_r = SR.ap[0:1, 0:512]; q_r = SR.ap[0:1, 512:1024]
            v_r = SR.ap[0:1, 1024:1536]; r_r = SR.ap[0:1, 1536:2048]
            b_r = SR.ap[0:1, 2048:2560]
            t1 = em.op("vector", lambda e: e.tensor_scalar(
                m_r, Psum.ap[0:1, :], 1.0 / D, None, ALU.mult),
                after=Psum.rdeps() + SR.wdeps())
            Psum.read(t1)
            t2 = em.op("vector", lambda e: e.tensor_scalar(
                q_r, Psq.ap[0:1, :], 1.0 / D, None, ALU.mult),
                after=Psq.rdeps() + [t1])
            Psq.read(t2)
            t3 = em.op("vector", lambda e: e.tensor_tensor(
                v_r, m_r, m_r, ALU.mult), after=[t1, t2])
            t4 = em.op("vector", lambda e: e.tensor_tensor(
                v_r, q_r, v_r, ALU.subtract), after=[t3])
            t5 = em.op("scalar", lambda e: e.activation(
                r_r, v_r, AF.Ln, bias=self.EPS1.ap[:], scale=1.0),
                after=[t4] + self.EPS1.rdeps())
            t6 = em.op("scalar", lambda e: e.activation(
                r_r, r_r, AF.Exp, scale=-0.5), after=[t5])
            t7 = em.op("vector", lambda e: e.tensor_tensor(
                b_r, m_r, r_r, ALU.mult), after=[t6])
            t8 = em.op("vector", lambda e: e.tensor_scalar(
                b_r, b_r, -1.0, None, ALU.mult), after=[t7])
            ta = em.op("vector", lambda e: e.tensor_copy(
                AB.ap[0:1, 0:512], r_r), after=[t6] + AB.wdeps())
            tb = em.op("vector", lambda e: e.tensor_copy(
                AB.ap[0:1, 512:1024], b_r), after=[t8] + AB.wdeps())
            SR.wrote(tb)
            AB.wrote(tb)
            tba = em.op("tensor", lambda e: e.matmul(
                Pa.ap[:], self.OB1.ap[:], AB.ap[0:1, 0:512], start=True, stop=True),
                after=[ta, tb] + Pa.wdeps() + self.OB1.rdeps())
            Pa.wrote(tba)
            tbb = em.op("tensor", lambda e: e.matmul(
                Pb.ap[:], self.OB1.ap[:], AB.ap[0:1, 512:1024], start=True, stop=True),
                after=[ta, tb] + Pb.wdeps())
            Pb.wrote(tbb)
            v1 = em.op("vector", lambda e: e.tensor_copy(
                ABS_.ap[:, 0:512], Pa.ap[:]), after=Pa.rdeps() + ABS_.wdeps() + [tba])
            Pa.read(v1); ABS_.wrote(v1)
            v2 = em.op("vector", lambda e: e.tensor_copy(
                ABS_.ap[:, 512:1024], Pb.ap[:]), after=Pb.rdeps() + ABS_.wdeps() + [tbb])
            Pb.read(v2); ABS_.wrote(v2)
            for kc in range(NC):
                tmp = TMP[(kc % 2) + 2 * (tt % 2)]
                eng = "vector" if kc % 2 == 0 else "gpsimd"
                u1 = em.op(eng, lambda e, t=tmp, k=kc, c=col: e.tensor_tensor(
                    t.ap[:], self.XR[k].ap[:, c:c + 512], ABS_.ap[:, 0:512], ALU.mult),
                    after=self.XR[kc].wdeps() + ABS_.rdeps() + tmp.wdeps() + [v1])
                tmp.wrote(u1)
                ABS_.read(u1)
                u2 = em.op(eng, lambda e, t=tmp, k=kc, c=col: e.tensor_tensor(
                    self.XR[k].ap[:, c:c + 512], t.ap[:], ABS_.ap[:, 512:1024], ALU.add),
                    after=[u1, v2] + ABS_.rdeps())
                self.XR[kc].wrote(u2)
                tmp.read(u2)
                ABS_.read(u2)
                if not last:
                    u3 = em.op(eng, lambda e, k=kc, c=col: e.tensor_copy(
                        self.XB[k].ap[:, c:c + 512], self.XR[k].ap[:, c:c + 512]),
                        after=[u2] + self.XB[kc].wdeps())
                    self.XB[kc].wrote(u3)
                    self.XR[kc].read(u3)
        if not last:
            em.barrier()
        es.close()


    def l0_attn(self):
        em, nc = self.em, self.nc
        slopes = _slopes16()
        esA = ExitStack()
        wq = self.dram_in("Wqh", [NC, 128, 1024])
        wk = self.dram_in("Wkh", [NC, 128, 1024])
        m01d = self.dram_in("m01", [128, 4 * 512], BF16)
        augq = self.dram_in("augq", [H, 4, T], BF16)
        augk = self.dram_in("augk", [H, 4, T], BF16)
        bspd = self.dram_in("biasp", [128, H * 19])

        M01 = self.sb("M01", [128, 4 * 512], BF16, esA)
        M01.wrote(em.dma("sync", lambda e: e.dma_start(out=M01.ap[:], in_=m01d)))
        BSP = self.sb("BSP", [128, H * 19], F32, esA)
        BSP.wrote(em.dma("sync", lambda e: e.dma_start(out=BSP.ap[:], in_=bspd)))

        VTbuf = self.sb("VT", [128, 16 * (H * 65)], BF16, esA)
        CTX = [self.sb(f"CTX{cc}", [128, T], BF16, esA) for cc in range(NC)]
        QA = [self.sb(f"QA{i}", [128, T], BF16, esA) for i in range(2)]
        KA = [self.sb(f"KA{i}", [128, T], BF16, esA) for i in range(2)]
        WQB = [self.sb("WQB0", [128, 1024], BF16, esA)]
        WKB = [self.sb("WKB0", [128, 1024], BF16, esA)]
        PB = [self.sb(f"PBt{i}", [128, 512], BF16, esA) for i in range(3)]

        esV = ExitStack()
        WV = self.sb("WVbf", [128, NC * 512], BF16, esV)
        wvd = self.dram_in("Wvh", [2, 128, NC * 512])
        VT = VTbuf
        tko = em.op("vector", lambda e: e.memset(
            VT.ap[:].rearrange("p (a b) -> p a b", b=65)[:, :, 64:65], 1.0),
            after=VT.wdeps())
        VT.wrote(tko)
        vtt = []
        for half in range(2):
            tw = em.dma("gpsimd", lambda e, hf=half: e.dma_start(
                out=WV.ap[:], in_=wvd[hf]), after=WV.wdeps())
            WV.wrote(tw)
            for tile in range(16):
                Pv = self.P[tile % 2]
                mt = None
                for kc in range(NC):
                    mt = em.op("tensor", lambda e, k=kc, ti=tile, P=Pv: e.matmul(
                        P.ap[:], self.XB[k].ap[:, ti * 128:(ti + 1) * 128],
                        WV.ap[:, k * 512:(k + 1) * 512],
                        start=(k == 0), stop=(k == NC - 1)),
                        after=(self.XB[kc].rdeps() + WV.rdeps() +
                               (Pv.wdeps() if kc == 0 else [])))
                    self.XB[kc].read(mt)
                    WV.read(mt)
                Pv.wrote(mt)
                dst = VT.ap[:].rearrange("p (a b) -> p a b", b=65)[
                    :, tile * H + half * 8: tile * H + half * 8 + 8, 0:64]
                tv = em.op("vector", lambda e, d=dst, P=Pv: e.tensor_copy(
                    d, P.ap[:].rearrange("p (a b) -> p a b", b=64)),
                    after=Pv.rdeps() + [tko] + VT.wdeps())
                Pv.read(tv)
                vtt.append(tv)
        VT.wrote(vtt[-1])
        em.barrier()
        esV.close()
        VT = VTbuf
        esA2 = ExitStack()
        DENR = [self.sb("DENR0", [1, T], F32, esA2)]
        RBR = [self.sb("RBR0", [1, T], BF16, esA2)]

        Pq = Pk = self.P[4]
        Psc = [self.P[0], self.P[1], self.P[5]]
        Pcx = [self.P[2], self.P[3]]
        Pbc2 = [self.P[6], self.P[7]]

        pbi = [0]
        norm_jobs = []

        def prefetch_w(hp):
            wqb, wkb = WQB[0], WKB[0]
            tq = em.dma("gpsimd", lambda e, p=hp, b=wqb: e.dma_start(
                out=b.ap[:], in_=wq[p]), after=wqb.wdeps())
            wqb.wrote(tq)
            tk = em.dma("gpsimd", lambda e, p=hp, b=wkb: e.dma_start(
                out=b.ap[:], in_=wk[p]), after=wkb.wdeps())
            wkb.wrote(tk)

        def prefetch_bufs(hp, i):
            qa2, ka2 = [QA[0], QA[1]], [KA[0], KA[1]]
            h = 2 * hp + i
            qeng = "sync" if i == 0 else "scalar"
            ta = em.dma(qeng, lambda e, hh=h, b=qa2[i]: e.dma_start(
                out=b.ap[64:68, :], in_=augq[hh]), after=qa2[i].wdeps())
            qa2[i].wrote(ta)
            tb = em.dma(qeng, lambda e, hh=h, b=ka2[i]: e.dma_start(
                out=b.ap[64:68, :], in_=augk[hh]), after=ka2[i].wdeps())
            ka2[i].wrote(tb)

        for hp in range(H // 2):
            qa2 = [QA[0], QA[1]]
            ka2 = [KA[0], KA[1]]
            wqb, wkb = WQB[0], WKB[0]
            prefetch_w(hp)
            prefetch_bufs(hp, 0)
            prefetch_bufs(hp, 1)
            for (bufs, wbuf, Ps) in ((qa2, wqb, Pq), (ka2, wkb, Pk)):
                for tt in range(NTT):
                    col = tt * 512
                    mm = None
                    for kc in range(NC):
                        mm = em.op("tensor", lambda e, k=kc, w=wbuf, P=Ps, c=col: e.matmul(
                            P.ap[:], w.ap[:, k * 128:(k + 1) * 128],
                            self.XB[k].ap[:, c:c + 512],
                            start=(k == 0), stop=(k == NC - 1)),
                            after=(wbuf.rdeps() + self.XB[kc].rdeps() +
                                   (Ps.wdeps() if kc == 0 else [])))
                        wbuf.read(mm)
                        self.XB[kc].read(mm)
                    Ps.wrote(mm)
                    ev0 = em.op("vector", lambda e, P=Ps, b=bufs[0], c=col: e.tensor_copy(
                        b.ap[0:64, c:c + 512], P.ap[0:64, :]),
                        after=Ps.rdeps() + bufs[0].wdeps())
                    Ps.read(ev0)
                    bufs[0].wrote(ev0)
                    ev1 = em.op("vector", lambda e, P=Ps, b=bufs[1], c=col: e.tensor_copy(
                        b.ap[0:64, c:c + 512], P.ap[64:128, :]),
                        after=Ps.rdeps() + bufs[1].wdeps())
                    Ps.read(ev1)
                    bufs[1].wrote(ev1)
            for i in range(2):
                h = 2 * hp + i
                qa, ka = qa2[i], ka2[i]
                for qt in range(NTT):
                    qcol = qt * 512
                    pcx = Pcx[qt % 2]
                    kts = list(range(0, min(4 * qt + 4, 16)))
                    pend = []
                    def emit_S(kt):
                        psc = Psc[pbi[0] % 3]
                        pb = PB[pbi[0] % 3]
                        pbi[0] += 1
                        tmm = em.op("tensor", lambda e, P=psc, k=kt, q=qcol,
                                    kb=ka, qb=qa: e.matmul(
                            P.ap[:], kb.ap[0:68, k * 128:(k + 1) * 128],
                            qb.ap[0:68, q:q + 512], start=True, stop=True),
                            after=ka.rdeps() + qa.rdeps() + psc.wdeps())
                        ka.read(tmm)
                        qa.read(tmm)
                        psc.wrote(tmm)
                        dv = 4 * qt - kt
                        bcol = h * 19 + (dv + 3)
                        tex = em.op("scalar", lambda e, P=psc, b=pb, bc=bcol: e.activation(
                            b.ap[:], P.ap[:], AF.Exp, scale=0.125,
                            bias=BSP.ap[:, bc:bc + 1]),
                            after=psc.rdeps() + pb.wdeps() + BSP.rdeps())
                        psc.read(tex)
                        pb.wrote(tex)
                        j = kt - 4 * qt
                        if j >= 0:
                            tmk = em.op("vector", lambda e, b=pb, jj=j: e.tensor_tensor(
                                b.ap[:], b.ap[:], M01.ap[:, jj * 512:(jj + 1) * 512],
                                ALU.mult), after=pb.wdeps() + M01.rdeps())
                            M01.read(tmk)
                            pb.wrote(tmk)
                        pend.append((kt, pb))
                    def emit_ctx(idx):
                        kt, pb = pend[idx]
                        first = (idx == 0)
                        last = (idx == len(kts) - 1)
                        tcx = em.op("tensor", lambda e, P=pcx, b=pb, k=kt, hh=h,
                                    f=first, l=last: e.matmul(
                            P.ap[0:65, :],
                            VT.ap[:, (k * H + hh) * 65:(k * H + hh) * 65 + 65],
                            b.ap[:], start=f, stop=l),
                            after=pb.rdeps() + VT.rdeps() + (pcx.wdeps() if first else []))
                        pb.read(tcx)
                        VT.read(tcx)
                        if last:
                            pcx.wrote(tcx)
                    emit_S(kts[0])
                    if len(kts) > 1:
                        emit_S(kts[1])
                    for idx in range(len(kts)):
                        if idx + 2 < len(kts):
                            emit_S(kts[idx + 2])
                        emit_ctx(idx)
                    denr = DENR[0]
                    td = em.op("vector", lambda e, P=pcx, d=denr, q=qcol: e.tensor_copy(
                        d.ap[0:1, q:q + 512], P.ap[64:65, :]),
                        after=pcx.rdeps() + denr.wdeps())
                    pcx.read(td)
                    denr.wrote(td)
                    cc = h // 2
                    rbase = (h % 2) * 64
                    te = em.op("vector", lambda e, P=pcx, ch=cc, rb=rbase, q=qcol: e.tensor_copy(
                        CTX[ch].ap[rb:rb + 64, q:q + 512], P.ap[0:64, :]),
                        after=pcx.rdeps() + CTX[cc].wdeps())
                    pcx.read(te)
                    CTX[cc].wrote(te)
                    norm_jobs.append((h, qt, hp % 2))
                if True:
                    denr = DENR[0]
                    rbr = RBR[0]
                    tl = em.op("scalar", lambda e, d=denr: e.activation(
                        d.ap[:], d.ap[:], AF.Ln), after=denr.wdeps())
                    denr.wrote(tl)
                    tx = em.op("scalar", lambda e, d=denr, r=rbr: e.activation(
                        r.ap[:], d.ap[:], AF.Exp, scale=-1.0),
                        after=[tl] + rbr.wdeps())
                    denr.read(tx)
                    rbr.wrote(tx)
                    cc = h // 2
                    rbase = (h % 2) * 64
                    for qt in range(NTT):
                        qcol = qt * 512
                        pbc = Pbc2[qt % 2]
                        tbc = em.op("tensor", lambda e, r=rbr, P=pbc, q=qcol: e.matmul(
                            P.ap[:], self.OB1.ap[:], r.ap[0:1, q:q + 512],
                            start=True, stop=True),
                            after=[tx] + pbc.wdeps())
                        rbr.read(tbc)
                        pbc.wrote(tbc)
                        tn = em.op("vector", lambda e, ch=cc, rb=rbase, q=qcol, P=pbc: e.tensor_tensor(
                            CTX[ch].ap[rb:rb + 64, q:q + 512],
                            CTX[ch].ap[rb:rb + 64, q:q + 512], P.ap[0:64, :],
                            ALU.mult), after=CTX[cc].wdeps() + pbc.rdeps() + [tbc])
                        pbc.read(tn)
                        CTX[cc].wrote(tn)
        em.barrier()

        esA2.close()

        esO = ExitStack()
        WO = self.sb("WObf", [128, NC * 512], BF16, esO)
        wod = self.dram_in("Woh", [2, 128, NC * 512])
        for oh in range(2):
            two = em.dma("gpsimd", lambda e, o=oh: e.dma_start(
                out=WO.ap[:], in_=wod[o]), after=WO.wdeps())
            WO.wrote(two)
            for tt in range(NTT):
                col = tt * 512
                for oq in range(4):
                    oc = oh * 4 + oq
                    Po = self.P[oq + (tt % 2) * 4]
                    mm = None
                    for cc in range(NC):
                        mm = em.op("tensor", lambda e, c=cc, o=oq, P=Po, cl=col: e.matmul(
                            P.ap[:], WO.ap[:, c * 512 + o * 128: c * 512 + o * 128 + 128],
                            CTX[c].ap[:, cl:cl + 512], start=(c == 0), stop=(c == NC - 1)),
                            after=(WO.rdeps() + CTX[cc].rdeps() +
                                   (Po.wdeps() if cc == 0 else [])))
                        WO.read(mm)
                        CTX[cc].read(mm)
                    Po.wrote(mm)
                    ts = em.op("vector", lambda e, P=Po, o=oc, cl=col: e.scalar_tensor_tensor(
                        self.XR[o].ap[:, cl:cl + 512], P.ap[:], self.CP.ap[:, o:o + 1],
                        self.XR[o].ap[:, cl:cl + 512], ALU.add, ALU.add),
                        after=Po.rdeps() + self.XR[oc].wdeps() + self.CP.rdeps())
                    Po.read(ts)
                    self.XR[oc].wrote(ts)
        em.barrier()
        esO.close()
        esA.close()



    def topk_threshold(self, tag, srow_buf, es):
        em, nc = self.em, self.nc
        scr = nc.dram_tensor(f"scr_{tag}", [T], F32, kind="Internal").ap()
        S2D = self.sb(f"{tag}s2d", [128, 16], F32, es)
        t1 = em.dma("gpsimd", lambda e: e.dma_start(
            out=scr.rearrange("(a b) -> a b", a=1), in_=srow_buf.ap[:]),
            after=srow_buf.rdeps())
        srow_buf.read(t1)
        t2 = em.dma("gpsimd", lambda e: e.dma_start(
            out=S2D.ap[:], in_=scr.rearrange("(p f) -> p f", p=128)),
            after=[t1] + S2D.wdeps())
        S2D.wrote(t2)
        LO = self.sb(f"{tag}lo", [1, 1], F32, es)
        HI = self.sb(f"{tag}hi", [1, 1], F32, es)
        OF1 = self.sb(f"{tag}of1", [1, 128], F32, es)
        TROW = self.sb(f"{tag}trow", [1, 256], F32, es)
        TMIX = self.sb(f"{tag}tmix", [1, 16], F32, es)
        SPAN = self.sb(f"{tag}span", [1, 1], F32, es)
        C01 = self.sb(f"{tag}c01", [128, 256], F32, es)
        PCF = self.sb(f"{tag}pcf", [128, 16], F32, es)
        PC = self.sb(f"{tag}pc", [128, 16], BF16, es)
        CNT = self.sb(f"{tag}cnt", [1, 16], F32, es)
        G = self.sb(f"{tag}g", [1, 16], F32, es)
        LO.wrote(em.op("vector", lambda e: e.memset(LO.ap[:], -512.0)))
        HI.wrote(em.op("vector", lambda e: e.memset(HI.ap[:], 512.0)))
        OF1.wrote(em.op("vector", lambda e: e.memset(OF1.ap[:], 1.0)))
        Pt = Pc = self.P[5]
        srep = S2D.ap[:].rearrange("p (a b) -> p a b", a=1).to_broadcast([128, 16, 16])
        def emit_round(it):
            s1 = em.op("vector", lambda e: e.tensor_tensor(
                SPAN.ap[:], HI.ap[:], LO.ap[:], ALU.subtract),
                after=HI.rdeps() + LO.rdeps() + SPAN.wdeps())
            SPAN.wrote(s1)
            s2 = em.op("vector", lambda e: e.tensor_tensor(
                TROW.ap[:], self.BIS.ap[0:1, 0:256], SPAN.ap[:].to_broadcast([1, 256]),
                ALU.mult), after=[s1] + self.BIS.rdeps() + TROW.wdeps())
            SPAN.read(s2); TROW.wrote(s2)
            s3 = em.op("vector", lambda e: e.tensor_tensor(
                TROW.ap[:], TROW.ap[:], LO.ap[:].to_broadcast([1, 256]), ALU.add),
                after=[s2] + LO.rdeps())
            LO.read(s3); TROW.wrote(s3)
            b1 = em.op("tensor", lambda e: e.matmul(
                Pt.ap[:, 0:256], OF1.ap[:], TROW.ap[:], start=True, stop=True),
                after=[s3] + OF1.rdeps() + Pt.wdeps())
            TROW.read(b1); Pt.wrote(b1)
            b2 = em.op("vector", lambda e: e.tensor_tensor(
                C01.ap[:], srep, Pt.ap[:, 0:256].rearrange("p (a b) -> p a b", a=16),
                ALU.is_gt), after=S2D.rdeps() + Pt.rdeps() + C01.wdeps())
            S2D.read(b2); Pt.read(b2); C01.wrote(b2)
            b3 = em.op("vector", lambda e: e.tensor_reduce(
                PCF.ap[:], C01.ap[:].rearrange("p (a b) -> p a b", a=16),
                axis=AX.X, op=ALU.add), after=[b2] + PCF.wdeps())
            C01.read(b3); PCF.wrote(b3)
            b4 = em.op("vector", lambda e: e.tensor_copy(PC.ap[:], PCF.ap[:]),
                       after=[b3] + PC.wdeps())
            PCF.read(b4); PC.wrote(b4)
            b5 = em.op("tensor", lambda e: e.matmul(
                Pc.ap[0:1, 0:16], self.OC.ap[:], PC.ap[:], start=True, stop=True),
                after=[b4] + self.OC.rdeps() + Pc.wdeps())
            PC.read(b5); Pc.wrote(b5)
            b6 = em.op("scalar", lambda e: e.copy(CNT.ap[:], Pc.ap[0:1, 0:16]),
                       after=Pc.rdeps() + CNT.wdeps())
            Pc.read(b6); CNT.wrote(b6)
            b7 = em.op("vector", lambda e: e.tensor_scalar(
                G.ap[:], CNT.ap[:], 1023.5, None, ALU.is_gt),
                after=[b6] + G.wdeps())
            CNT.read(b7); G.wrote(b7)
            tauv = TROW.ap[0:1, :].rearrange("o (a b) -> o a b", b=16)[:, :, 0]
            c1 = em.op("vector", lambda e, tv=tauv: e.tensor_tensor(
                TMIX.ap[:], tv, LO.ap[:].to_broadcast([1, 16]), ALU.subtract),
                after=[s3, b7] + LO.rdeps() + TMIX.wdeps())
            TROW.read(c1); LO.read(c1); TMIX.wrote(c1)
            c2 = em.op("vector", lambda e: e.tensor_tensor(
                TMIX.ap[:], TMIX.ap[:], G.ap[:], ALU.mult), after=[c1] + G.rdeps())
            G.read(c2); TMIX.wrote(c2)
            c3 = em.op("vector", lambda e: e.tensor_tensor(
                TMIX.ap[:], TMIX.ap[:], LO.ap[:].to_broadcast([1, 16]), ALU.add),
                after=[c2] + LO.rdeps())
            LO.read(c3); TMIX.wrote(c3)
            c4 = em.op("vector", lambda e: e.tensor_reduce(
                LO.ap[:], TMIX.ap[:], axis=AX.X, op=ALU.max),
                after=[c3] + LO.wdeps())
            TMIX.read(c4); LO.wrote(c4)
            d1 = em.op("vector", lambda e, tv=tauv: e.tensor_tensor(
                TMIX.ap[:], tv, HI.ap[:].to_broadcast([1, 16]), ALU.subtract),
                after=[s3, c4] + HI.rdeps() + TMIX.wdeps())
            TROW.read(d1); HI.read(d1); TMIX.wrote(d1)
            d2 = em.op("vector", lambda e: e.tensor_scalar(
                G.ap[:], G.ap[:], -1.0, 1.0, ALU.mult, ALU.add), after=G.wdeps() + [c2])
            G.wrote(d2)
            d3 = em.op("vector", lambda e: e.tensor_tensor(
                TMIX.ap[:], TMIX.ap[:], G.ap[:], ALU.mult), after=[d1, d2])
            G.read(d3); TMIX.wrote(d3)
            d4 = em.op("vector", lambda e: e.tensor_tensor(
                TMIX.ap[:], TMIX.ap[:], HI.ap[:].to_broadcast([1, 16]), ALU.add),
                after=[d3] + HI.rdeps())
            HI.read(d4); TMIX.wrote(d4)
            d5 = em.op("vector", lambda e: e.tensor_reduce(
                HI.ap[:], TMIX.ap[:], axis=AX.X, op=ALU.min),
                after=[d4] + HI.wdeps())
            TMIX.read(d5); HI.wrote(d5)
        rounds = [lambda it=it: emit_round(it) for it in range(9)]
        return LO, HI, rounds

    def l0_ffn(self):
        em, nc = self.em, self.nc
        es = ExitStack()
        w1d = self.dram_in("W1g0", [NC, 128, 4 * 1024])
        w2d = self.dram_in("W2g0", [NC, 128, 4 * 1024])
        HB = self.sb("HB0", [128, FC * 512], BF16, es)
        W1B = [self.sb(f"W1B{i}", [128, 4096], BF16, es) for i in range(2)]
        W2B = [self.sb(f"W2B{i}", [128, 4096], BF16, es) for i in range(2)]
        SROW = self.sb("SROW0", [1, T], F32, es)
        GROW = self.sb("GROW0", [1, T], F32, es)
        GRB = self.sb("GRB0", [1, T], BF16, es)
        WRB = self.sb("WRB0", [128, 8], BF16, es)
        GB = self.sb("GB0", [128, 512], BF16, es)
        TMP = [self.sb(f"F0T{i}", [128, 512], F32, es) for i in range(2)]

        t = em.op("vector", lambda e: e.tensor_copy(WRB.ap[:], self.CP.ap[:, 48:56]),
                  after=self.CP.rdeps() + WRB.wdeps())
        WRB.wrote(t)
        Pr = self.P[4]
        for tt in range(NTT):
            col = tt * 512
            mm = None
            for kc in range(NC):
                mm = em.op("tensor", lambda e, k=kc, c=col: e.matmul(
                    Pr.ap[0:1, :], WRB.ap[:, k:k + 1], self.XB[k].ap[:, c:c + 512],
                    start=(k == 0), stop=(k == NC - 1)),
                    after=(WRB.rdeps() + self.XB[kc].rdeps() +
                           (Pr.wdeps() if kc == 0 else [])))
                WRB.read(mm); self.XB[kc].read(mm)
            Pr.wrote(mm)
            te = em.op("scalar", lambda e, c=col: e.copy(
                SROW.ap[0:1, c:c + 512], Pr.ap[0:1, :]),
                after=Pr.rdeps() + SROW.wdeps())
            Pr.read(te); SROW.wrote(te)
        LO, HI, rounds = self.topk_threshold("tk0", SROW, es)
        GBF = self.sb("GBF0", [128, T], BF16, es)

        def emit_gates():
            g1 = em.op("vector", lambda e: e.tensor_tensor(
                GROW.ap[:], SROW.ap[:], LO.ap[:].to_broadcast([1, T]), ALU.is_gt),
                after=SROW.rdeps() + LO.rdeps() + GROW.wdeps())
            SROW.read(g1); LO.read(g1); GROW.wrote(g1)
            g2 = em.op("scalar", lambda e: e.activation(
                GRB.ap[:], SROW.ap[:], AF.Sigmoid), after=SROW.rdeps() + GRB.wdeps())
            SROW.read(g2); GRB.wrote(g2)
            g3 = em.op("vector", lambda e: e.tensor_tensor(
                GRB.ap[:], GRB.ap[:], GROW.ap[:], ALU.mult),
                after=[g1, g2] + GRB.wdeps())
            GROW.read(g3); GRB.wrote(g3)
            for tt in range(NTT):
                col = tt * 512
                Pg = self.P[5]
                bg = em.op("tensor", lambda e, c=col: e.matmul(
                    Pg.ap[:], self.OB1.ap[:], GRB.ap[0:1, c:c + 512], start=True, stop=True),
                    after=GRB.rdeps() + self.OB1.rdeps() + Pg.wdeps())
                GRB.read(bg); Pg.wrote(bg)
                bg2 = em.op("vector", lambda e, c=col: e.tensor_copy(
                    GBF.ap[:, c:c + 512], Pg.ap[:]), after=Pg.rdeps() + GBF.wdeps())
                Pg.read(bg2); GBF.wrote(bg2)
        hooks0 = rounds + [emit_gates]

        def evac0(oc, tt, P, GBFb):
            col = tt * 512
            tmp = TMP[oc % 2]
            e1 = em.op("vector", lambda e, Pp=P, o=oc, t=tmp, c=col: e.scalar_tensor_tensor(
                t.ap[:], Pp.ap[:], self.CP.ap[:, 40 + o:41 + o], GBFb.ap[:, c:c + 512],
                ALU.add, ALU.mult),
                after=P.rdeps() + GBFb.rdeps() + tmp.wdeps() + self.CP.rdeps())
            P.read(e1); GBFb.read(e1); tmp.wrote(e1)
            e2 = em.op("vector", lambda e, o=oc, t=tmp, c=col: e.scalar_tensor_tensor(
                self.XR[o].ap[:, c:c + 512], self.XR[o].ap[:, c:c + 512], 2.0,
                t.ap[:], ALU.mult, ALU.add),
                after=[e1] + self.XR[oc].wdeps())
            tmp.read(e2); self.XR[oc].wrote(e2)

        self.ffn_core("f0", w1d, w2d, 8, 40, AF.Gelu_apprx_tanh, HB, W1B, W2B, GBF, evac0,
                      hooks=hooks0)
        em.barrier()
        es.close()




    def ffn_core(self, tag, w1d, w2d, b1_off, b2_off, act, HB, W1B, W2B, GBF, evac,
                 hooks=None):
        em = self.em
        HBc = [Buf(HB.ap[:, h * 512:(h + 1) * 512]) for h in range(FC)]
        for tt in range(NTT):
            col = tt * 512

            def emit_A(hc):
                hg, hq = hc // 4, hc % 4
                w1b = W1B[hg % 2]
                if hq == 0:
                    tw = em.dma("gpsimd", lambda e, g=hg, b=w1b: e.dma_start(
                        out=b.ap[:], in_=w1d[g]), after=w1b.wdeps())
                    w1b.wrote(tw)
                Ph = self.P[6 + hc % 2]
                mm = None
                for kc in range(NC):
                    mm = em.op("tensor", lambda e, k=kc, q=hq, P=Ph, b=w1b, c=col: e.matmul(
                        P.ap[:], b.ap[:, q * 1024 + k * 128: q * 1024 + k * 128 + 128],
                        self.XB[k].ap[:, c:c + 512],
                        start=(k == 0), stop=(k == NC - 1)),
                        after=(w1b.rdeps() + self.XB[kc].rdeps() +
                               (Ph.wdeps() if kc == 0 else [])))
                    w1b.read(mm); self.XB[kc].read(mm)
                Ph.wrote(mm)
                ev = em.op("scalar", lambda e, P=Ph, h=hc: e.activation(
                    HBc[h].ap, P.ap[:], act,
                    bias=self.CP.ap[:, b1_off + h:b1_off + h + 1], scale=1.0),
                    after=Ph.rdeps() + HBc[hc].wdeps() + self.CP.rdeps())
                Ph.read(ev); HBc[hc].wrote(ev)

            def emit_B1(hc):
                hg, hq = hc // 4, hc % 4
                w2b = W2B[hg % 2]
                if hq == 0:
                    tw = em.dma("gpsimd", lambda e, g=hg, b=w2b: e.dma_start(
                        out=b.ap[:], in_=w2d[g]), after=w2b.wdeps())
                    w2b.wrote(tw)
                for oc in range(5):
                    P = self.P[oc]
                    mm = em.op("tensor", lambda e, q=hq, o=oc, b=w2b, h=hc, Pp=P: e.matmul(
                        Pp.ap[:], b.ap[:, q * 1024 + o * 128: q * 1024 + o * 128 + 128],
                        HBc[h].ap, start=(h == 0), stop=(h == FC - 1)),
                        after=(w2b.rdeps() + HBc[hc].rdeps() +
                               (P.wdeps() if hc == 0 else [])))
                    w2b.read(mm); HBc[hc].read(mm)
                    if hc == FC - 1:
                        P.wrote(mm)

            def emit_B2(hc):
                hg, hq = hc // 4, hc % 4
                w2b = W2B[hg % 2]
                if hq == 0:
                    tw = em.dma("gpsimd", lambda e, g=hg, b=w2b: e.dma_start(
                        out=b.ap[:], in_=w2d[g]), after=w2b.wdeps())
                    w2b.wrote(tw)
                for oi, oc in enumerate((5, 6, 7)):
                    P = self.P[oi]
                    mm = em.op("tensor", lambda e, q=hq, o=oc, b=w2b, h=hc, Pp=P: e.matmul(
                        Pp.ap[:], b.ap[:, q * 1024 + o * 128: q * 1024 + o * 128 + 128],
                        HBc[h].ap, start=(h == 0), stop=(h == FC - 1)),
                        after=(w2b.rdeps() + HBc[hc].rdeps() +
                               (P.wdeps() if hc == 0 else [])))
                    w2b.read(mm); HBc[hc].read(mm)
                    if hc == FC - 1:
                        P.wrote(mm)

            emit_A(0)
            emit_A(1)
            for hc in range(FC):
                if hc + 2 < FC:
                    emit_A(hc + 2)
                if hooks and hc % 2 == 0:
                    hooks.pop(0)()
                emit_B1(hc)
            while hooks:
                hooks.pop(0)()
            for oc in (0, 1, 2):
                evac(oc, tt, self.P[oc], GBF)
            for hc in range(FC):
                emit_B2(hc)
            for oc in (3, 4):
                evac(oc, tt, self.P[oc], GBF)
            for oi, oc in enumerate((5, 6, 7)):
                evac(oc, tt, self.P[oi], GBF)


    def l1_attn(self, debug=False):
        em, nc = self.em, self.nc
        if debug:
            self.dbg_ek = nc.dram_tensor("dbg_ek", [D, T], F32, kind="ExternalOutput").ap()
            self.dbg_y = nc.dram_tensor("dbg_y", [D, T], F32, kind="ExternalOutput").ap()
            self.dbg_a = nc.dram_tensor("dbg_a", [D, T], F32, kind="ExternalOutput").ap()
        esL = ExitStack()
        ktd = self.dram_in("KTh", [128, NC * 1024])
        vtd = self.dram_in("VTh", [128, NC * 1024])
        rtd = self.dram_in("RTh", [128, NC * 1024])
        otd = self.dram_in("OTh", [128, NC * 1024])

        DCOL = self.sb("DCOL", [128, NC], F32, esL)
        EUCOL = self.sb("EUCOL", [128, NC], F32, esL)
        CAB = self.sb("CAB", [128, NC], F32, esL)
        CBB = self.sb("CBB", [128, NC], F32, esL)
        TCOL = self.sb("TCOL", [128, 1], F32, esL)
        t0 = em.op("vector", lambda e: e.memset(CAB.ap[:], 0.0), after=CAB.wdeps())
        CAB.wrote(t0)
        t0 = em.op("vector", lambda e: e.memset(CBB.ap[:], 0.0), after=CBB.wdeps())
        CBB.wrote(t0)
        for cc in range(NC):
            a1 = em.op("scalar", lambda e, c=cc: e.activation(
                TCOL.ap[:], self.CP.ap[:, 56 + c:57 + c], AF.Exp),
                after=self.CP.rdeps() + TCOL.wdeps())
            TCOL.wrote(a1)
            a2 = em.op("vector", lambda e: e.tensor_scalar(
                TCOL.ap[:], TCOL.ap[:], -1.0, None, ALU.mult), after=TCOL.wdeps())
            TCOL.wrote(a2)
            a3 = em.op("scalar", lambda e, c=cc: e.activation(
                DCOL.ap[:, c:c + 1], TCOL.ap[:], AF.Exp),
                after=[a2] + DCOL.wdeps())
            TCOL.read(a3); DCOL.wrote(a3)
            a4 = em.op("scalar", lambda e, c=cc: e.activation(
                EUCOL.ap[:, c:c + 1], self.CP.ap[:, 64 + c:65 + c], AF.Exp),
                after=self.CP.rdeps() + EUCOL.wdeps())
            EUCOL.wrote(a4)

        esH = ExitStack()
        NB_ = 4
        def alias(k, lo):
            b = Buf(self.XB[k].ap[:, lo * 512:(lo + 1) * 512])
            b.w = self.XB[k].w
            b.r = list(self.XB[k].r)
            return b
        XK = [[alias(k, 0 + i) for k in range(NC)] for i in range(2)]
        XV = [[alias(k, 2 + i) for k in range(NC)] for i in range(2)]
        XRr = [[self.sb(f"XRr{i}{k}", [128, 512], BF16, esH) for k in range(NC)]
               for i in range(2)]
        DD = [self.sb(f"DDl{i}", [128, 512], F32, esH) for i in range(2)]
        EKb = [self.sb(f"EKb{i}", [128, 512], F32, esH) for i in range(2)]
        VV = [self.sb(f"VVb{i}", [128, 512], F32, esH) for i in range(2)]
        EKV = [self.sb(f"EKVb{i}", [128, 512], F32, esH) for i in range(2)]
        Ab = [self.sb(f"Ab{i}", [128, 513], F32, esH) for i in range(2)]
        Bb = [self.sb(f"Bb{i}", [128, 513], F32, esH) for i in range(2)]
        NUM = [self.sb(f"NUMb{i}", [128, 512], F32, esH) for i in range(2)]
        SRB = [self.sb(f"SRBb{i}", [128, 512], BF16, esH) for i in range(2)]
        LT = [self.sb("LTl0", [128, 512], F32, esH)]
        RWK = [self.sb(f"RWK{i}", [128, NC * 512], BF16, esH) for i in range(2)]
        KTA = self.sb("KTA", [128, NC * 1024], BF16, esH)
        VTA = self.sb("VTA", [128, NC * 1024], BF16, esH)
        RTA = [self.sb(f"RTA{i}", [128, 1024], BF16, esH) for i in range(2)]
        OTB = self.sb("OTB", [128, NC * 512], BF16, esH)
        for (tb, dsrc) in ((KTA, ktd), (VTA, vtd)):
            tw = em.dma("gpsimd", lambda e, b=tb, s=dsrc: e.dma_start(
                out=b.ap[:], in_=s), after=tb.wdeps())
            tb.wrote(tw)
        rt_bf = nc.dram_tensor("rt_bf", [128, NC * 1024], BF16, kind="Internal").ap()
        ot_bf = nc.dram_tensor("ot_bf", [128, NC * 1024], BF16, kind="Internal").ap()
        RTC = Buf(rt_bf)
        OTC = Buf(ot_bf)
        RTC.wrote(em.dma("gpsimd", lambda e: e.dma_start(out=rt_bf, in_=rtd)))
        OTC.wrote(em.dma("gpsimd", lambda e: e.dma_start(out=ot_bf, in_=otd)))

        def emit_lerps(blk):
            pp = blk % 2
            h0 = blk * 512
            for kc in range(NC):
                dd = DD[kc % 2]
                if blk == 0:
                    d1 = em.op("vector", lambda e, k=kc, d=dd: e.tensor_tensor(
                        d.ap[:, 0:511], self.XR[k].ap[:, 1:512],
                        self.XR[k].ap[:, 0:511], ALU.subtract),
                        after=self.XR[kc].rdeps() + dd.wdeps())
                else:
                    d1 = em.op("vector", lambda e, k=kc, d=dd, hh=h0: e.tensor_tensor(
                        d.ap[:], self.XR[k].ap[:, hh:hh + 512],
                        self.XR[k].ap[:, hh - 1:hh + 511], ALU.subtract),
                        after=self.XR[kc].rdeps() + dd.wdeps())
                self.XR[kc].read(d1); dd.wrote(d1)
                for li, (dst, coff) in enumerate(((XK, 72), (XV, 80), (XRr, 88))):
                    buf = dst[pp][kc]
                    if li < 2:
                        if blk == 0:
                            s1 = em.op("vector", lambda e, b=buf, k=kc, co=coff, d=dd: e.scalar_tensor_tensor(
                                b.ap[:, 1:512], d.ap[:, 0:511],
                                self.CP.ap[:, co + k:co + k + 1],
                                self.XR[k].ap[:, 0:511], ALU.mult, ALU.add),
                                after=[d1] + buf.wdeps() + self.CP.rdeps())
                            buf.wrote(s1)
                            s2 = em.op("vector", lambda e, b=buf, k=kc, co=coff: e.tensor_scalar(
                                b.ap[:, 0:1], self.XR[k].ap[:, 0:1],
                                self.CP.ap[:, co + k:co + k + 1], None, ALU.mult),
                                after=buf.wdeps() + self.XR[kc].rdeps())
                            buf.wrote(s2)
                        else:
                            s1 = em.op("vector", lambda e, b=buf, k=kc, co=coff, d=dd, hh=h0: e.scalar_tensor_tensor(
                                b.ap[:], d.ap[:], self.CP.ap[:, co + k:co + k + 1],
                                self.XR[k].ap[:, hh - 1:hh + 511], ALU.mult, ALU.add),
                                after=[d1] + buf.wdeps() + self.CP.rdeps())
                            buf.wrote(s1)
                    else:
                        lt = LT[0]
                        if blk == 0:
                            p1 = em.op("gpsimd", lambda e, k=kc, co=coff, d=dd, t=lt: e.tensor_tensor(
                                t.ap[:, 0:511], d.ap[:, 0:511],
                                self.CP.ap[:, co + k:co + k + 1].to_broadcast([128, 511]),
                                ALU.mult), after=[d1] + lt.wdeps() + self.CP.rdeps())
                            dd.read(p1); lt.wrote(p1)
                            p2 = em.op("gpsimd", lambda e, b=buf, k=kc, t=lt: e.tensor_tensor(
                                b.ap[:, 1:512], t.ap[:, 0:511],
                                self.XR[k].ap[:, 0:511], ALU.add),
                                after=[p1] + buf.wdeps())
                            lt.read(p2); buf.wrote(p2)
                            p3 = em.op("gpsimd", lambda e, b=buf, k=kc, co=coff: e.tensor_tensor(
                                b.ap[:, 0:1], self.XR[k].ap[:, 0:1],
                                self.CP.ap[:, co + k:co + k + 1], ALU.mult),
                                after=buf.wdeps() + self.XR[kc].rdeps())
                            buf.wrote(p3)
                        else:
                            p1 = em.op("gpsimd", lambda e, k=kc, co=coff, d=dd, t=lt: e.tensor_tensor(
                                t.ap[:], d.ap[:],
                                self.CP.ap[:, co + k:co + k + 1].to_broadcast([128, 512]),
                                ALU.mult), after=[d1] + lt.wdeps() + self.CP.rdeps())
                            dd.read(p1); lt.wrote(p1)
                            p2 = em.op("gpsimd", lambda e, b=buf, k=kc, t=lt, hh=h0: e.tensor_tensor(
                                b.ap[:], t.ap[:],
                                self.XR[k].ap[:, hh - 1:hh + 511], ALU.add),
                                after=[p1] + buf.wdeps())
                            lt.read(p2); buf.wrote(p2)
                    dd.read(buf.w)
                    self.XR[kc].read(buf.w)

        def emit_proj(blk, cc):
            pp = blk % 2
            sp = cc % 2
            ekb, vv, ekv_b, srb = EKb[sp], VV[sp], EKV[sp], SRB[sp]
            rta = RTA[(blk * NC + cc) % 2]
            tr = em.dma("sync", lambda e, b=rta, c=cc: e.dma_start(
                out=b.ap[:], in_=rt_bf[:, c * 1024:(c + 1) * 1024]),
                after=rta.wdeps() + RTC.rdeps())
            RTC.read(tr)
            rta.wrote(tr)
            for pi, (wb, xin, outbuf, actf) in enumerate((
                    (KTA, XK[pp], ekb, AF.Exp), (VTA, XV[pp], vv, AF.Identity),
                    (rta, XRr[pp], srb, AF.Sigmoid))):
                Pp = self.P[pi + 3 * sp]
                mm = None
                for kc in range(NC):
                    off = (cc * 1024 if wb in (KTA, VTA) else 0)
                    mm = em.op("tensor", lambda e, k=kc, w=wb, P=Pp, xi=xin, o=off: e.matmul(
                        P.ap[:], w.ap[:, o + k * 128: o + k * 128 + 128],
                        xi[k].ap[:], start=(k == 0), stop=(k == NC - 1)),
                        after=(wb.rdeps() + xin[kc].rdeps() +
                               (Pp.wdeps() if kc == 0 else [])))
                    wb.read(mm); xin[kc].read(mm)
                Pp.wrote(mm)
                ev = em.op("scalar", lambda e, P=Pp, o=outbuf, f=actf: e.activation(
                    o.ap[:], P.ap[:], f),
                    after=Pp.rdeps() + outbuf.wdeps())
                Pp.read(ev); outbuf.wrote(ev)
            ekv = em.op("gpsimd", lambda e, a=ekv_b, x_=ekb, y_=vv: e.tensor_tensor(
                a.ap[:], x_.ap[:], y_.ap[:], ALU.mult),
                after=ekb.rdeps() + vv.rdeps() + ekv_b.wdeps())
            ekb.read(ekv); vv.read(ekv); ekv_b.wrote(ekv)

        def emit_scan(blk, cc):
            pp = blk % 2
            sp = cc % 2
            ekb, vv, ekv_b = EKb[sp], VV[sp], EKV[sp]
            ab, bb, num, srb = Ab[sp], Bb[sp], NUM[sp], SRB[sp]
            rwk = RWK[blk % 2]
            ca = em.op("vector", lambda e, c=cc, a=ab: e.tensor_copy(
                a.ap[:, 0:1], CAB.ap[:, c:c + 1]),
                after=CAB.rdeps() + ab.wdeps())
            CAB.read(ca); ab.wrote(ca)
            sa = em.op("vector", lambda e, c=cc, a=ab, x_=ekv_b: e.tensor_tensor_scan(
                a.ap[:, 1:513], DCOL.ap[:, c:c + 1].to_broadcast([128, 512]),
                x_.ap[:], CAB.ap[:, c:c + 1], ALU.mult, ALU.add),
                after=[ca] + ekv_b.rdeps() + DCOL.rdeps() + CAB.rdeps() + ab.wdeps())
            DCOL.read(sa); CAB.read(sa); ekv_b.read(sa); ab.wrote(sa)
            ua = em.op("vector", lambda e, c=cc, a=ab: e.tensor_copy(
                CAB.ap[:, c:c + 1], a.ap[:, 512:513]),
                after=[sa] + CAB.wdeps())
            ab.read(ua); CAB.wrote(ua)
            cb = em.op("vector", lambda e, c=cc, a=bb: e.tensor_copy(
                a.ap[:, 0:1], CBB.ap[:, c:c + 1]),
                after=CBB.rdeps() + bb.wdeps())
            CBB.read(cb); bb.wrote(cb)
            sb_ = em.op("vector", lambda e, c=cc, a=bb, x_=ekb: e.tensor_tensor_scan(
                a.ap[:, 1:513], DCOL.ap[:, c:c + 1].to_broadcast([128, 512]),
                x_.ap[:], CBB.ap[:, c:c + 1], ALU.mult, ALU.add),
                after=[cb] + ekb.rdeps() + DCOL.rdeps() + CBB.rdeps() + bb.wdeps())
            DCOL.read(sb_); CBB.read(sb_); ekb.read(sb_); bb.wrote(sb_)
            ub = em.op("vector", lambda e, c=cc, a=bb: e.tensor_copy(
                CBB.ap[:, c:c + 1], a.ap[:, 512:513]),
                after=[sb_] + CBB.wdeps())
            bb.read(ub); CBB.wrote(ub)
            n1 = em.op("vector", lambda e, c=cc, a=ab, x_=ekv_b, n=num: e.scalar_tensor_tensor(
                n.ap[:], x_.ap[:], EUCOL.ap[:, c:c + 1], a.ap[:, 0:512],
                ALU.mult, ALU.add),
                after=ekv_b.rdeps() + [sa] + EUCOL.rdeps() + num.wdeps())
            ekv_b.read(n1); ab.read(n1); EUCOL.read(n1); num.wrote(n1)
            n2 = em.op("vector", lambda e, c=cc, a=bb, x_=ekb, o=ekv_b: e.scalar_tensor_tensor(
                o.ap[:], x_.ap[:], EUCOL.ap[:, c:c + 1], a.ap[:, 0:512],
                ALU.mult, ALU.add),
                after=[sb_, n1] + ekb.rdeps() + EUCOL.rdeps() + ekv_b.wdeps())
            ekb.read(n2); bb.read(n2); EUCOL.read(n2); ekv_b.wrote(n2)
            r1 = em.op("scalar", lambda e, x_=ekv_b, o=ekb: e.activation(
                o.ap[:], x_.ap[:], AF.Ln), after=[n2] + ekb.wdeps())
            ekv_b.read(r1); ekb.wrote(r1)
            r2 = em.op("scalar", lambda e, o=ekb: e.activation(
                o.ap[:], o.ap[:], AF.Exp, scale=-1.0), after=ekb.wdeps())
            ekb.wrote(r2)
            y1 = em.op("gpsimd", lambda e, n=num, o=ekb: e.tensor_tensor(
                n.ap[:], n.ap[:], o.ap[:], ALU.mult),
                after=[r2] + num.wdeps())
            ekb.read(y1); num.wrote(y1)
            y2 = em.op("gpsimd", lambda e, c=cc, n=num, s=srb, r=rwk: e.tensor_tensor(
                r.ap[:, c * 512:(c + 1) * 512], n.ap[:], s.ap[:], ALU.mult),
                after=[y1] + srb.rdeps() + rwk.wdeps())
            num.read(y2); srb.read(y2); rwk.wrote(y2)

        def emit_outproj(blk):
            h0 = blk * 512
            rwk = RWK[blk % 2]
            for oh in range(2):
                tw = em.dma("sync", lambda e, o=oh: e.dma_start(
                    out=OTB.ap[:].rearrange("p (c m) -> p c m", m=512),
                    in_=ot_bf.rearrange("p (c g m) -> p c g m", g=2, m=512)[:, :, o]),
                    after=OTB.wdeps() + OTC.rdeps())
                OTC.read(tw)
                OTB.wrote(tw)
                for oi in range(4):
                    oc = oh * 4 + oi
                    Po = self.P[6 + oi % 2]
                    mm = None
                    for cc in range(NC):
                        mm = em.op("tensor", lambda e, c=cc, o=oi, Pp=Po, r=rwk: e.matmul(
                            Pp.ap[:], OTB.ap[:, c * 512 + o * 128: c * 512 + o * 128 + 128],
                            r.ap[:, c * 512:(c + 1) * 512],
                            start=(c == 0), stop=(c == NC - 1)),
                            after=(OTB.rdeps() + rwk.rdeps() +
                                   (Po.wdeps() if cc == 0 else [])))
                        OTB.read(mm); rwk.read(mm)
                    Po.wrote(mm)
                    ts = em.op("vector", lambda e, Pp=Po, o=oc, hh=h0: e.tensor_tensor(
                        self.XR[o].ap[:, hh:hh + 512], self.XR[o].ap[:, hh:hh + 512],
                        Pp.ap[:], ALU.add),
                        after=Po.rdeps() + self.XR[oc].wdeps())
                    Po.read(ts); self.XR[oc].wrote(ts)

        emit_lerps(0)
        for blk in range(NB_):
            if blk + 1 < NB_:
                emit_lerps(blk + 1)
            emit_proj(blk, 0)
            for cc in range(1, NC):
                emit_proj(blk, cc)
                if cc == 4 and blk > 0:
                    emit_outproj(blk - 1)
                emit_scan(blk, cc - 1)
            emit_scan(blk, NC - 1)
        emit_outproj(NB_ - 1)
        em.barrier()
        esH.close()
        esL.close()




    def l1_ffn(self):
        em, nc = self.em, self.nc
        es = ExitStack()
        w1d = self.dram_in("W1g1", [NC, 128, 4 * 1024])
        w2d = self.dram_in("W2g1", [NC, 128, 4 * 1024])
        HB = self.sb("HB1", [128, FC * 512], BF16, es)
        W1B = [self.sb(f"X1B{i}", [128, 4096], BF16, es) for i in range(2)]
        W2B = [self.sb(f"X2B{i}", [128, 4096], BF16, es) for i in range(2)]
        SROW = self.sb("SROW1", [1, T], F32, es)
        GRB = self.sb("GRB1", [1, T], BF16, es)
        WRB = self.sb("WRB1", [128, 8], BF16, es)
        GB = self.sb("GB1", [128, 512], BF16, es)
        TMP = [self.sb(f"F1T{i}", [128, 512], F32, es) for i in range(2)]

        t = em.op("vector", lambda e: e.tensor_copy(WRB.ap[:], self.CP.ap[:, 136:144]),
                  after=self.CP.rdeps() + WRB.wdeps())
        WRB.wrote(t)
        Pr = self.P[4]
        for tt in range(NTT):
            col = tt * 512
            mm = None
            for kc in range(NC):
                mm = em.op("tensor", lambda e, k=kc, c=col: e.matmul(
                    Pr.ap[0:1, :], WRB.ap[:, k:k + 1], self.XB[k].ap[:, c:c + 512],
                    start=(k == 0), stop=(k == NC - 1)),
                    after=(WRB.rdeps() + self.XB[kc].rdeps() +
                           (Pr.wdeps() if kc == 0 else [])))
                WRB.read(mm); self.XB[kc].read(mm)
            Pr.wrote(mm)
            te = em.op("scalar", lambda e, c=col: e.activation(
                SROW.ap[0:1, c:c + 512], Pr.ap[0:1, :], AF.Identity,
                bias=self.CP.ap[0:1, 144:145], scale=1.0),
                after=Pr.rdeps() + SROW.wdeps() + self.CP.rdeps())
            Pr.read(te); SROW.wrote(te)
        LO, HI, rounds = self.topk_threshold("tk1", SROW, es)
        GBF = self.sb("GBF1", [128, T], BF16, es)

        def emit_gates():
            g1 = em.op("vector", lambda e: e.tensor_tensor(
                GRB.ap[:], SROW.ap[:], HI.ap[:].to_broadcast([1, T]), ALU.is_gt),
                after=SROW.rdeps() + HI.rdeps() + GRB.wdeps())
            SROW.read(g1); HI.read(g1); GRB.wrote(g1)
            for tt in range(NTT):
                col = tt * 512
                Pg = self.P[5]
                bg = em.op("tensor", lambda e, c=col: e.matmul(
                    Pg.ap[:], self.OB1.ap[:], GRB.ap[0:1, c:c + 512], start=True, stop=True),
                    after=GRB.rdeps() + self.OB1.rdeps() + Pg.wdeps())
                GRB.read(bg); Pg.wrote(bg)
                bg2 = em.op("vector", lambda e, c=col: e.tensor_copy(
                    GBF.ap[:, c:c + 512], Pg.ap[:]), after=Pg.rdeps() + GBF.wdeps())
                Pg.read(bg2); GBF.wrote(bg2)
        hooks1 = rounds + [emit_gates]

        def evac1(oc, tt, P, GBFb):
            col = tt * 512
            tmp = TMP[oc % 2]
            e1 = em.op("vector", lambda e, Pp=P, o=oc, t=tmp, c=col: e.scalar_tensor_tensor(
                t.ap[:], Pp.ap[:], self.CP.ap[:, 128 + o:129 + o],
                self.XR[o].ap[:, c:c + 512], ALU.add, ALU.subtract),
                after=P.rdeps() + self.XR[oc].wdeps() + tmp.wdeps() + self.CP.rdeps())
            P.read(e1); self.XR[oc].read(e1); tmp.wrote(e1)
            e2 = em.op("vector", lambda e, t=tmp, c=col: e.tensor_tensor(
                t.ap[:], t.ap[:], GBFb.ap[:, c:c + 512], ALU.mult),
                after=[e1] + GBFb.rdeps() + tmp.wdeps())
            GBFb.read(e2); tmp.wrote(e2)
            e3 = em.op("vector", lambda e, o=oc, t=tmp, c=col: e.scalar_tensor_tensor(
                self.XR[o].ap[:, c:c + 512], self.XR[o].ap[:, c:c + 512], 2.0,
                t.ap[:], ALU.mult, ALU.add),
                after=[e2] + self.XR[oc].wdeps())
            tmp.read(e3); self.XR[oc].wrote(e3)

        self.ffn_core("f1", w1d, w2d, 96, 128, AF.Gelu, HB, W1B, W2B, GBF, evac1,
                      hooks=hooks1)
        em.barrier()
        es.close()



    def finish(self):
        with self.nc.Block() as block:
            self.em.play(block)
        self.es.close()
        return self.nc



def _tile_wcol(W, ncol):
    K, M = W.shape
    return np.ascontiguousarray(
        W.reshape(K // 128, 128, M // ncol, ncol).transpose(2, 1, 0, 3))


def _tile_wrow(W):
    K, M = W.shape
    return np.ascontiguousarray(W.reshape(K // 128, 128, M).transpose(1, 0, 2))


def host_prep(inputs):
    slopes = _slopes16()
    d = {}
    d["Wqh"] = _tile_wcol(np.asarray(inputs["L0_attn_Wq"]), 128).reshape(NC, 128, 1024)
    d["Wkh"] = _tile_wcol(np.asarray(inputs["L0_attn_Wk"]), 128).reshape(NC, 128, 1024)
    Wv = np.asarray(inputs["L0_attn_Wv"])
    d["Wvh"] = np.ascontiguousarray(
        Wv.reshape(NC, 128, 2, 512).transpose(2, 1, 0, 3)).reshape(2, 128, NC * 512)
    Wo = np.asarray(inputs["L0_attn_Wo"])
    d["Woh"] = np.ascontiguousarray(
        Wo.reshape(NC, 128, 2, 512).transpose(2, 1, 0, 3)).reshape(2, 128, NC * 512)

    cp = np.zeros((128, 160), np.float32)
    def packcol(vec, c0):
        v = np.asarray(vec).reshape(-1, 128).T
        cp[:, c0:c0 + v.shape[1]] = v
        return c0 + v.shape[1]
    packcol(inputs["L0_attn_bo"], 0)
    packcol(inputs["L0_ffn_b1"], 8)
    packcol(inputs["L0_ffn_b2"], 40)
    packcol(inputs["L0_ffn_w_router"], 48)
    packcol(inputs["L1_attn_time_decay"], 56)
    packcol(inputs["L1_attn_time_first"], 64)
    packcol(inputs["L1_attn_time_mix_k"], 72)
    packcol(inputs["L1_attn_time_mix_v"], 80)
    packcol(inputs["L1_attn_time_mix_r"], 88)
    packcol(inputs["L1_ffn_b1"], 96)
    packcol(inputs["L1_ffn_b2"], 128)
    packcol(inputs["L1_ffn_w_r"].reshape(-1), 136)
    cp[0, 144] = float(np.asarray(inputs["L1_ffn_b_r"]).reshape(-1)[0])
    d["cpack"] = cp

    def grp4(W):
        K, M = W.shape
        return np.ascontiguousarray(
            W.reshape(K // 128, 128, M // 512, 4, 128)
             .transpose(2, 1, 3, 0, 4)
             .reshape(M // 512, 128, 4 * K))

    def grp4row(W):
        K, M = W.shape
        return np.ascontiguousarray(
            W.reshape(K // 512, 4, 128, M)
             .transpose(0, 2, 1, 3)
             .reshape(K // 512, 128, 4 * M))

    d["W1g0"] = grp4(np.asarray(inputs["L0_ffn_W1"]))
    d["W2g0"] = grp4row(np.asarray(inputs["L0_ffn_W2"]))

    def colblocks(Wt):
        return np.ascontiguousarray(
            _tile_wcol(Wt, 128).reshape(NC, 128, 1024)
            .transpose(1, 0, 2)).reshape(128, NC * 1024)

    d["KTh"] = colblocks(np.asarray(inputs["L1_attn_Wk"]).T)
    d["VTh"] = colblocks(np.asarray(inputs["L1_attn_Wv"]).T)
    d["RTh"] = colblocks(np.asarray(inputs["L1_attn_Wr"]).T)
    d["OTh"] = _tile_wrow(np.asarray(inputs["L1_attn_Wo"]).T).reshape(128, NC * 1024)
    d["W1g1"] = grp4(np.asarray(inputs["L1_ffn_W1"]))
    d["W2g1"] = grp4row(np.asarray(inputs["L1_ffn_W2"]))

    qloc = np.tile(np.arange(512, dtype=np.float64), NTT)
    kloc = np.tile(np.arange(128, dtype=np.float64), 16)
    augq = np.zeros((H, 4, T), np.float64)
    augk = np.zeros((H, 4, T), np.float64)
    biasp = np.zeros((128, H * 19), np.float32)
    for h in range(H):
        s = slopes[h]
        vq = -s * qloc
        vqh = vq.astype(BF).astype(np.float64)
        augq[h, 0] = 1.0
        augq[h, 1] = 1.0
        augq[h, 2] = vqh
        augq[h, 3] = vq - vqh
        vk = s * kloc
        vkh = vk.astype(BF).astype(np.float64)
        augk[h, 0] = vkh
        augk[h, 1] = vk - vkh
        augk[h, 2] = 1.0
        augk[h, 3] = 1.0
        for dv in range(-3, 16):
            biasp[:, h * 19 + dv + 3] = -s * (128.0 * dv) / 8.0
    d["augq"] = augq.astype(BF)
    d["augk"] = augk.astype(BF)
    d["biasp"] = biasp
    d["bisrow"] = ((np.arange(256) // 16 + 1) / 17.0).astype(np.float32).reshape(1, 256)

    m01 = np.zeros((128, 4, 512), np.float32)
    p = np.arange(128)[:, None]
    f = np.arange(512)[None, :]
    for j in range(4):
        m01[:, j, :] = (p <= f - 128 * j)
    d["m01"] = m01.reshape(128, 4 * 512).astype(BF)
    return d


_PROG_CACHE = {}


def _get_prog(blocks):
    key = tuple(blocks)
    if key not in _PROG_CACHE:
        b = Build(blocks)
        b.setup_consts()
        b.load_stream("xT")
        if "a0" in blocks:
            b.l0_attn()
            b.layernorm("ln0")
        if "f0" in blocks:
            b.l0_ffn()
            b.layernorm("ln1")
        if "a1" in blocks:
            b.l1_attn(debug=("dbg" in blocks))
            b.layernorm("ln2")
        if "f1" in blocks:
            b.l1_ffn()
            b.layernorm("ln3", last=("a0" in blocks))
        b.store_stream("yT")
        nc = b.finish()
        _PROG_CACHE[key] = (nc, sorted(b.need))
    return _PROG_CACHE[key]


def kernel(**inputs):
    x = np.asarray(inputs["L0_attn_x"])
    B = x.shape[0]
    hp = host_prep(inputs)

    nc, need = _get_prog(["a0", "f0", "a1", "f1"])
    in_maps = []
    for b in range(B):
        m = {k: hp[k] for k in need if k in hp}
        m["xT"] = np.ascontiguousarray(x[b].T)
        in_maps.append(m)
    res = run_bass_kernel_spmd(nc, in_maps, core_ids=list(range(B)))
    out = np.stack([np.asarray(r["yT"]).T for r in res.results])
    return out.astype(np.float32)


# version 21
# speedup vs baseline: 1.1704x; 1.1704x over previous
import math
from contextlib import ExitStack

import numpy as np
import ml_dtypes
import concourse.bass as bass
import concourse.mybir as mybir
from concourse.bass_utils import run_bass_kernel_spmd

F32 = mybir.dt.float32
BF16 = mybir.dt.bfloat16
AF = mybir.ActivationFunctionType
ALU = mybir.AluOpType
AX = mybir.AxisListType

D = 1024
T = 2048
NC = 8
H = 16
HD = 64
FF = 4096
FC = FF // 128
NTT = 4
BF = ml_dtypes.bfloat16

EPS = 1e-5


def _slopes16():
    s = 2.0 ** (-0.5)
    return [s ** (i + 1) for i in range(16)]



ENGS = ("sync", "scalar", "vector", "tensor", "gpsimd")


class Em:
    def __init__(self, nc, es):
        self.nc = nc
        self.q = []
        self.sems = {}
        self.dmasems = {
            "sync": [f"dqs{i}" for i in range(4)],
            "gpsimd": [f"dqg{i}" for i in range(8)],
            "scalar": ["dqc0"],
        }
        names = list(ENGS)
        for lst in self.dmasems.values():
            names += lst
        for name in names:
            self.sems[name] = es.enter_context(nc.semaphore("S_" + name))
        self.cnt = {k: 0 for k in self.sems}
        self.dmaidx = {k: 0 for k in self.dmasems}
        self.last_wait = {}

    def _push(self, eng, fn, after, sem_name, amt):
        waits = []
        seen = {}
        for tok in after:
            if tok is None:
                continue
            s, v = tok
            if v > seen.get(s, -1):
                seen[s] = v
        for s, v in seen.items():
            k = (eng, s)
            if self.last_wait.get(k, -1) >= v:
                continue
            self.last_wait[k] = v
            waits.append((self.sems[s], v))
        inc = None
        tok = None
        if sem_name is not None:
            self.cnt[sem_name] += amt
            tok = (sem_name, self.cnt[sem_name])
            inc = (self.sems[sem_name], amt)
        self.q.append((eng, waits, fn, inc))
        return tok

    def op(self, eng, fn, after=()):
        return self._push(eng, fn, after, eng, 1)

    def dma(self, eng, fn, after=()):
        lst = self.dmasems[eng]
        i = self.dmaidx[eng] % len(lst)
        self.dmaidx[eng] += 1
        sem = lst[i]
        after = list(after)
        if self.cnt[sem] > 0:
            after.append((sem, self.cnt[sem]))
        return self._push(eng, fn, after, sem, 16)

    def barrier(self):
        toks = [(s, self.cnt[s]) for s in self.cnt if self.cnt[s] > 0]
        for eng in ENGS:
            self._push(eng, None, toks, None, 0)

    def play(self, block):
        for eng_name in ENGS:
            dec = getattr(block, eng_name)

            def body(e, _eng=eng_name):
                for (eng, waits, fn, inc) in self.q:
                    if eng != _eng:
                        continue
                    for (sem, v) in waits:
                        e.wait_ge(sem, v)
                    if fn is None:
                        continue
                    instr = fn(e)
                    if inc is not None:
                        instr.then_inc(*inc)

            dec(body)


class Buf:

    def __init__(self, ap):
        self.ap = ap
        self.w = None
        self.r = []

    def wdeps(self):
        return ([self.w] if self.w else []) + self.r

    def rdeps(self):
        return [self.w] if self.w else []

    def wrote(self, tok):
        self.w = tok
        self.r = []

    def read(self, tok):
        self.r.append(tok)



class Build:

    def __init__(self, blocks):
        self.blocks = blocks
        nc = bass.Bass("TRN2", target_bir_lowering=False)
        self.nc = nc
        self.es = ExitStack()
        self.em = Em(nc, self.es)
        self.din = {}
        self.need = set()

    def dram_in(self, name, shape, dtype=F32):
        ap = self.nc.dram_tensor(name, shape, dtype, kind="ExternalInput").ap()
        self.din[name] = ap
        self.need.add(name)
        return ap

    def sb(self, name, shape, dtype=F32, es=None):
        return Buf((es or self.es).enter_context(
            self.nc.sbuf_tensor(name, shape, dtype)))

    def ps(self, name, es=None):
        return Buf((es or self.es).enter_context(
            self.nc.psum_tensor(name, [128, 512], F32)))


    def setup_consts(self):
        em, nc = self.em, self.nc
        self.CP = self.sb("CP", [128, 160])
        cp = self.dram_in("cpack", [128, 160])
        t = em.dma("sync", lambda e: e.dma_start(out=self.CP.ap[:], in_=cp))
        self.CP.wrote(t)
        self.OB1 = self.sb("OB1", [1, 128], BF16)
        self.OB1.wrote(em.op("vector", lambda e: e.memset(self.OB1.ap[:], 1.0)))
        self.OC = self.sb("OC", [128, 1], BF16)
        self.OC.wrote(em.op("vector", lambda e: e.memset(self.OC.ap[:], 1.0)))
        self.EPS1 = self.sb("EPS1", [1, 1], F32)
        self.EPS1.wrote(em.op("vector", lambda e: e.memset(self.EPS1.ap[:], EPS)))
        self.BIS = self.sb("BIS", [1, 256], F32)
        bisd = self.dram_in("bisrow", [1, 256])
        self.BIS.wrote(em.dma("sync", lambda e: e.dma_start(out=self.BIS.ap[:], in_=bisd)))
        self.P = [self.ps(f"P{i}") for i in range(8)]

    def load_stream(self, xname):
        em = self.em
        self.XR = [self.sb(f"XR{kc}", [128, T]) for kc in range(NC)]
        self.XB = [self.sb(f"XB{kc}", [128, T], BF16) for kc in range(NC)]
        xT = self.dram_in(xname, [D, T])
        for kc in range(NC):
            src = xT.rearrange("(c p) t -> c p t", p=128)[kc]
            self.XR[kc].wrote(em.dma(
                "sync", lambda e, s=src, d=self.XR[kc]: e.dma_start(out=d.ap[:], in_=s)))
            self.XB[kc].wrote(em.dma(
                "gpsimd", lambda e, s=src, d=self.XB[kc]: e.dma_start(out=d.ap[:], in_=s)))

    def store_stream(self, yname):
        em = self.em
        yT = self.nc.dram_tensor(yname, [D, T], F32, kind="ExternalOutput").ap()
        toks = []
        for kc in range(NC):
            dst = yT.rearrange("(c p) t -> c p t", p=128)[kc]
            toks.append(em.dma(
                "sync", lambda e, d=dst, s=self.XR[kc]: e.dma_start(out=d, in_=s.ap[:]),
                after=self.XR[kc].rdeps()))
        self.em._push("sync", None, toks, None, 0)

    def cast_weight(self, wname, shape, sbbuf, after=()):
        em = self.em
        src = self.din[wname] if wname in self.din else self.dram_in(wname, shape)
        t = em.dma("gpsimd",
                   lambda e: e.dma_start(out=sbbuf.ap[:], in_=src),
                   after=list(after) + sbbuf.wdeps())
        sbbuf.wrote(t)
        return t


    def layernorm(self, tag, last=False):
        em = self.em
        es = ExitStack()
        SQ = [self.sb(f"{tag}sq{i}", [128, 512], BF16, es) for i in range(4)]
        SR2 = [self.sb(f"{tag}sr{i}", [1, 512 * 6], F32, es) for i in range(2)]
        ABS2 = [self.sb(f"{tag}abs{i}", [128, 1024], F32, es) for i in range(2)]
        AB2 = [self.sb(f"{tag}ab{i}", [1, 512 * 2], BF16, es) for i in range(2)]
        TMP = [self.sb(f"{tag}tmp{i}", [128, 512], F32, es) for i in range(4)]
        PsumL = [self.P[6], self.P[4]]
        PsqL = [self.P[7], self.P[5]]
        PaL = [self.P[0], self.P[2]]
        PbL = [self.P[1], self.P[3]]
        for kc in range(NC):
            tcb = em.op("vector", lambda e, k=kc: e.tensor_copy(
                self.XB[k].ap[:], self.XR[k].ap[:]),
                after=self.XR[kc].rdeps() + self.XB[kc].wdeps())
            self.XB[kc].wrote(tcb)
            self.XR[kc].read(tcb)
        for tt in range(NTT):
            col = tt * 512
            Psum, Psq = PsumL[tt % 2], PsqL[tt % 2]
            Pa, Pb = PaL[tt % 2], PbL[tt % 2]
            for kc in range(NC):
                sq = SQ[(kc % 2) + 2 * (tt % 2)]
                tsq = em.op("scalar", lambda e, s=sq, k=kc, c=col: e.activation(
                    s.ap[:], self.XR[k].ap[:, c:c + 512], AF.Square),
                    after=self.XR[kc].rdeps() + sq.wdeps())
                sq.wrote(tsq)
                self.XR[kc].read(tsq)
                tm = em.op("tensor", lambda e, k=kc, c=col: e.matmul(
                    Psum.ap[0:1, :], self.OC.ap[:], self.XB[k].ap[:, c:c + 512],
                    start=(k == 0), stop=(k == NC - 1)),
                    after=self.XB[kc].rdeps() + self.OC.rdeps() +
                          (Psum.wdeps() if kc == 0 else []))
                self.XB[kc].read(tm)
                if kc == NC - 1:
                    Psum.wrote(tm)
                tq = em.op("tensor", lambda e, s=sq, k=kc: e.matmul(
                    Psq.ap[0:1, :], self.OC.ap[:], s.ap[:],
                    start=(k == 0), stop=(k == NC - 1)),
                    after=sq.rdeps() + (Psq.wdeps() if kc == 0 else []))
                sq.read(tq)
                if kc == NC - 1:
                    Psq.wrote(tq)
            SR = SR2[tt % 2]
            AB = AB2[tt % 2]
            ABS_ = ABS2[tt % 2]
            m_r = SR.ap[0:1, 0:512]; q_r = SR.ap[0:1, 512:1024]
            v_r = SR.ap[0:1, 1024:1536]; r_r = SR.ap[0:1, 1536:2048]
            b_r = SR.ap[0:1, 2048:2560]
            t1 = em.op("vector", lambda e: e.tensor_scalar(
                m_r, Psum.ap[0:1, :], 1.0 / D, None, ALU.mult),
                after=Psum.rdeps() + SR.wdeps())
            Psum.read(t1)
            t2 = em.op("vector", lambda e: e.tensor_scalar(
                q_r, Psq.ap[0:1, :], 1.0 / D, None, ALU.mult),
                after=Psq.rdeps() + [t1])
            Psq.read(t2)
            t3 = em.op("vector", lambda e: e.tensor_tensor(
                v_r, m_r, m_r, ALU.mult), after=[t1, t2])
            t4 = em.op("vector", lambda e: e.tensor_tensor(
                v_r, q_r, v_r, ALU.subtract), after=[t3])
            t5 = em.op("scalar", lambda e: e.activation(
                r_r, v_r, AF.Ln, bias=self.EPS1.ap[:], scale=1.0),
                after=[t4] + self.EPS1.rdeps())
            t6 = em.op("scalar", lambda e: e.activation(
                r_r, r_r, AF.Exp, scale=-0.5), after=[t5])
            t7 = em.op("vector", lambda e: e.tensor_tensor(
                b_r, m_r, r_r, ALU.mult), after=[t6])
            t8 = em.op("vector", lambda e: e.tensor_scalar(
                b_r, b_r, -1.0, None, ALU.mult), after=[t7])
            ta = em.op("vector", lambda e: e.tensor_copy(
                AB.ap[0:1, 0:512], r_r), after=[t6] + AB.wdeps())
            tb = em.op("vector", lambda e: e.tensor_copy(
                AB.ap[0:1, 512:1024], b_r), after=[t8] + AB.wdeps())
            SR.wrote(tb)
            AB.wrote(tb)
            tba = em.op("tensor", lambda e: e.matmul(
                Pa.ap[:], self.OB1.ap[:], AB.ap[0:1, 0:512], start=True, stop=True),
                after=[ta, tb] + Pa.wdeps() + self.OB1.rdeps())
            Pa.wrote(tba)
            tbb = em.op("tensor", lambda e: e.matmul(
                Pb.ap[:], self.OB1.ap[:], AB.ap[0:1, 512:1024], start=True, stop=True),
                after=[ta, tb] + Pb.wdeps())
            Pb.wrote(tbb)
            v1 = em.op("vector", lambda e: e.tensor_copy(
                ABS_.ap[:, 0:512], Pa.ap[:]), after=Pa.rdeps() + ABS_.wdeps() + [tba])
            Pa.read(v1); ABS_.wrote(v1)
            v2 = em.op("vector", lambda e: e.tensor_copy(
                ABS_.ap[:, 512:1024], Pb.ap[:]), after=Pb.rdeps() + ABS_.wdeps() + [tbb])
            Pb.read(v2); ABS_.wrote(v2)
            for kc in range(NC):
                tmp = TMP[(kc % 2) + 2 * (tt % 2)]
                eng = "vector" if kc % 2 == 0 else "gpsimd"
                u1 = em.op(eng, lambda e, t=tmp, k=kc, c=col: e.tensor_tensor(
                    t.ap[:], self.XR[k].ap[:, c:c + 512], ABS_.ap[:, 0:512], ALU.mult),
                    after=self.XR[kc].wdeps() + ABS_.rdeps() + tmp.wdeps() + [v1])
                tmp.wrote(u1)
                ABS_.read(u1)
                u2 = em.op(eng, lambda e, t=tmp, k=kc, c=col: e.tensor_tensor(
                    self.XR[k].ap[:, c:c + 512], t.ap[:], ABS_.ap[:, 512:1024], ALU.add),
                    after=[u1, v2] + ABS_.rdeps())
                self.XR[kc].wrote(u2)
                tmp.read(u2)
                ABS_.read(u2)
                u3 = em.op(eng, lambda e, k=kc, c=col: e.tensor_copy(
                    self.XB[k].ap[:, c:c + 512], self.XR[k].ap[:, c:c + 512]),
                    after=[u2] + self.XB[kc].wdeps())
                self.XB[kc].wrote(u3)
                self.XR[kc].read(u3)
        if not last:
            em.barrier()
        es.close()


    def l0_attn(self):
        em, nc = self.em, self.nc
        slopes = _slopes16()
        esA = ExitStack()
        wq = self.dram_in("Wqh", [NC, 128, 1024])
        wk = self.dram_in("Wkh", [NC, 128, 1024])
        m01d = self.dram_in("m01", [128, 4 * 512], BF16)
        augq = self.dram_in("augq", [H, 4, T], BF16)
        augk = self.dram_in("augk", [H, 4, T], BF16)
        bspd = self.dram_in("biasp", [128, H * 19])

        M01 = self.sb("M01", [128, 4 * 512], BF16, esA)
        M01.wrote(em.dma("sync", lambda e: e.dma_start(out=M01.ap[:], in_=m01d)))
        BSP = self.sb("BSP", [128, H * 19], F32, esA)
        BSP.wrote(em.dma("sync", lambda e: e.dma_start(out=BSP.ap[:], in_=bspd)))

        VTbuf = self.sb("VT", [128, 16 * (H * 65)], BF16, esA)
        CTX = [self.sb(f"CTX{cc}", [128, T], BF16, esA) for cc in range(NC)]
        QA = [self.sb(f"QA{i}", [128, T], BF16, esA) for i in range(2)]
        KA = [self.sb(f"KA{i}", [128, T], BF16, esA) for i in range(2)]
        WQB = [self.sb("WQB0", [128, 1024], BF16, esA)]
        WKB = [self.sb("WKB0", [128, 1024], BF16, esA)]
        PB = [self.sb(f"PBt{i}", [128, 512], BF16, esA) for i in range(3)]

        esV = ExitStack()
        WV = self.sb("WVbf", [128, NC * 512], BF16, esV)
        wvd = self.dram_in("Wvh", [2, 128, NC * 512])
        VT = VTbuf
        tko = em.op("vector", lambda e: e.memset(
            VT.ap[:].rearrange("p (a b) -> p a b", b=65)[:, :, 64:65], 1.0),
            after=VT.wdeps())
        VT.wrote(tko)
        vtt = []
        for half in range(2):
            tw = em.dma("gpsimd", lambda e, hf=half: e.dma_start(
                out=WV.ap[:], in_=wvd[hf]), after=WV.wdeps())
            WV.wrote(tw)
            for tile in range(16):
                Pv = self.P[tile % 2]
                mt = None
                for kc in range(NC):
                    mt = em.op("tensor", lambda e, k=kc, ti=tile, P=Pv: e.matmul(
                        P.ap[:], self.XB[k].ap[:, ti * 128:(ti + 1) * 128],
                        WV.ap[:, k * 512:(k + 1) * 512],
                        start=(k == 0), stop=(k == NC - 1)),
                        after=(self.XB[kc].rdeps() + WV.rdeps() +
                               (Pv.wdeps() if kc == 0 else [])))
                    self.XB[kc].read(mt)
                    WV.read(mt)
                Pv.wrote(mt)
                dst = VT.ap[:].rearrange("p (a b) -> p a b", b=65)[
                    :, tile * H + half * 8: tile * H + half * 8 + 8, 0:64]
                tv = em.op("vector", lambda e, d=dst, P=Pv: e.tensor_copy(
                    d, P.ap[:].rearrange("p (a b) -> p a b", b=64)),
                    after=Pv.rdeps() + [tko] + VT.wdeps())
                Pv.read(tv)
                vtt.append(tv)
        VT.wrote(vtt[-1])
        em.barrier()
        esV.close()
        VT = VTbuf
        esA2 = ExitStack()
        DENR = [self.sb("DENR0", [1, T], F32, esA2)]
        RBR = [self.sb("RBR0", [1, T], BF16, esA2)]

        Pq = Pk = self.P[4]
        Psc = [self.P[0], self.P[1], self.P[5]]
        Pcx = [self.P[2], self.P[3]]
        Pbc2 = [self.P[6], self.P[7]]

        pbi = [0]
        norm_jobs = []

        def prefetch_w(hp):
            wqb, wkb = WQB[0], WKB[0]
            tq = em.dma("gpsimd", lambda e, p=hp, b=wqb: e.dma_start(
                out=b.ap[:], in_=wq[p]), after=wqb.wdeps())
            wqb.wrote(tq)
            tk = em.dma("gpsimd", lambda e, p=hp, b=wkb: e.dma_start(
                out=b.ap[:], in_=wk[p]), after=wkb.wdeps())
            wkb.wrote(tk)

        def prefetch_bufs(hp, i):
            qa2, ka2 = [QA[0], QA[1]], [KA[0], KA[1]]
            h = 2 * hp + i
            qeng = "sync" if i == 0 else "scalar"
            ta = em.dma(qeng, lambda e, hh=h, b=qa2[i]: e.dma_start(
                out=b.ap[64:68, :], in_=augq[hh]), after=qa2[i].wdeps())
            qa2[i].wrote(ta)
            tb = em.dma(qeng, lambda e, hh=h, b=ka2[i]: e.dma_start(
                out=b.ap[64:68, :], in_=augk[hh]), after=ka2[i].wdeps())
            ka2[i].wrote(tb)

        for hp in range(H // 2):
            qa2 = [QA[0], QA[1]]
            ka2 = [KA[0], KA[1]]
            wqb, wkb = WQB[0], WKB[0]
            prefetch_w(hp)
            prefetch_bufs(hp, 0)
            prefetch_bufs(hp, 1)
            for (bufs, wbuf, Ps) in ((qa2, wqb, Pq), (ka2, wkb, Pk)):
                for tt in range(NTT):
                    col = tt * 512
                    mm = None
                    for kc in range(NC):
                        mm = em.op("tensor", lambda e, k=kc, w=wbuf, P=Ps, c=col: e.matmul(
                            P.ap[:], w.ap[:, k * 128:(k + 1) * 128],
                            self.XB[k].ap[:, c:c + 512],
                            start=(k == 0), stop=(k == NC - 1)),
                            after=(wbuf.rdeps() + self.XB[kc].rdeps() +
                                   (Ps.wdeps() if kc == 0 else [])))
                        wbuf.read(mm)
                        self.XB[kc].read(mm)
                    Ps.wrote(mm)
                    ev0 = em.op("vector", lambda e, P=Ps, b=bufs[0], c=col: e.tensor_copy(
                        b.ap[0:64, c:c + 512], P.ap[0:64, :]),
                        after=Ps.rdeps() + bufs[0].wdeps())
                    Ps.read(ev0)
                    bufs[0].wrote(ev0)
                    ev1 = em.op("vector", lambda e, P=Ps, b=bufs[1], c=col: e.tensor_copy(
                        b.ap[0:64, c:c + 512], P.ap[64:128, :]),
                        after=Ps.rdeps() + bufs[1].wdeps())
                    Ps.read(ev1)
                    bufs[1].wrote(ev1)
            for i in range(2):
                h = 2 * hp + i
                qa, ka = qa2[i], ka2[i]
                for qt in range(NTT):
                    qcol = qt * 512
                    pcx = Pcx[qt % 2]
                    kts = list(range(0, min(4 * qt + 4, 16)))
                    pend = []
                    def emit_S(kt):
                        psc = Psc[pbi[0] % 3]
                        pb = PB[pbi[0] % 3]
                        pbi[0] += 1
                        tmm = em.op("tensor", lambda e, P=psc, k=kt, q=qcol,
                                    kb=ka, qb=qa: e.matmul(
                            P.ap[:], kb.ap[0:68, k * 128:(k + 1) * 128],
                            qb.ap[0:68, q:q + 512], start=True, stop=True),
                            after=ka.rdeps() + qa.rdeps() + psc.wdeps())
                        ka.read(tmm)
                        qa.read(tmm)
                        psc.wrote(tmm)
                        dv = 4 * qt - kt
                        bcol = h * 19 + (dv + 3)
                        tex = em.op("scalar", lambda e, P=psc, b=pb, bc=bcol: e.activation(
                            b.ap[:], P.ap[:], AF.Exp, scale=0.125,
                            bias=BSP.ap[:, bc:bc + 1]),
                            after=psc.rdeps() + pb.wdeps() + BSP.rdeps())
                        psc.read(tex)
                        pb.wrote(tex)
                        j = kt - 4 * qt
                        if j >= 0:
                            tmk = em.op("vector", lambda e, b=pb, jj=j: e.tensor_tensor(
                                b.ap[:], b.ap[:], M01.ap[:, jj * 512:(jj + 1) * 512],
                                ALU.mult), after=pb.wdeps() + M01.rdeps())
                            M01.read(tmk)
                            pb.wrote(tmk)
                        pend.append((kt, pb))
                    def emit_ctx(idx):
                        kt, pb = pend[idx]
                        first = (idx == 0)
                        last = (idx == len(kts) - 1)
                        tcx = em.op("tensor", lambda e, P=pcx, b=pb, k=kt, hh=h,
                                    f=first, l=last: e.matmul(
                            P.ap[0:65, :],
                            VT.ap[:, (k * H + hh) * 65:(k * H + hh) * 65 + 65],
                            b.ap[:], start=f, stop=l),
                            after=pb.rdeps() + VT.rdeps() + (pcx.wdeps() if first else []))
                        pb.read(tcx)
                        VT.read(tcx)
                        if last:
                            pcx.wrote(tcx)
                    emit_S(kts[0])
                    if len(kts) > 1:
                        emit_S(kts[1])
                    for idx in range(len(kts)):
                        if idx + 2 < len(kts):
                            emit_S(kts[idx + 2])
                        emit_ctx(idx)
                    denr = DENR[0]
                    td = em.op("vector", lambda e, P=pcx, d=denr, q=qcol: e.tensor_copy(
                        d.ap[0:1, q:q + 512], P.ap[64:65, :]),
                        after=pcx.rdeps() + denr.wdeps())
                    pcx.read(td)
                    denr.wrote(td)
                    cc = h // 2
                    rbase = (h % 2) * 64
                    te = em.op("vector", lambda e, P=pcx, ch=cc, rb=rbase, q=qcol: e.tensor_copy(
                        CTX[ch].ap[rb:rb + 64, q:q + 512], P.ap[0:64, :]),
                        after=pcx.rdeps() + CTX[cc].wdeps())
                    pcx.read(te)
                    CTX[cc].wrote(te)
                    norm_jobs.append((h, qt, hp % 2))
                if True:
                    denr = DENR[0]
                    rbr = RBR[0]
                    tl = em.op("scalar", lambda e, d=denr: e.activation(
                        d.ap[:], d.ap[:], AF.Ln), after=denr.wdeps())
                    denr.wrote(tl)
                    tx = em.op("scalar", lambda e, d=denr, r=rbr: e.activation(
                        r.ap[:], d.ap[:], AF.Exp, scale=-1.0),
                        after=[tl] + rbr.wdeps())
                    denr.read(tx)
                    rbr.wrote(tx)
                    cc = h // 2
                    rbase = (h % 2) * 64
                    for qt in range(NTT):
                        qcol = qt * 512
                        pbc = Pbc2[qt % 2]
                        tbc = em.op("tensor", lambda e, r=rbr, P=pbc, q=qcol: e.matmul(
                            P.ap[:], self.OB1.ap[:], r.ap[0:1, q:q + 512],
                            start=True, stop=True),
                            after=[tx] + pbc.wdeps())
                        rbr.read(tbc)
                        pbc.wrote(tbc)
                        tn = em.op("vector", lambda e, ch=cc, rb=rbase, q=qcol, P=pbc: e.tensor_tensor(
                            CTX[ch].ap[rb:rb + 64, q:q + 512],
                            CTX[ch].ap[rb:rb + 64, q:q + 512], P.ap[0:64, :],
                            ALU.mult), after=CTX[cc].wdeps() + pbc.rdeps() + [tbc])
                        pbc.read(tn)
                        CTX[cc].wrote(tn)
        em.barrier()

        esA2.close()

        esO = ExitStack()
        WO = self.sb("WObf", [128, NC * 512], BF16, esO)
        wod = self.dram_in("Woh", [2, 128, NC * 512])
        for oh in range(2):
            two = em.dma("gpsimd", lambda e, o=oh: e.dma_start(
                out=WO.ap[:], in_=wod[o]), after=WO.wdeps())
            WO.wrote(two)
            for tt in range(NTT):
                col = tt * 512
                for oq in range(4):
                    oc = oh * 4 + oq
                    Po = self.P[oq + (tt % 2) * 4]
                    mm = None
                    for cc in range(NC):
                        mm = em.op("tensor", lambda e, c=cc, o=oq, P=Po, cl=col: e.matmul(
                            P.ap[:], WO.ap[:, c * 512 + o * 128: c * 512 + o * 128 + 128],
                            CTX[c].ap[:, cl:cl + 512], start=(c == 0), stop=(c == NC - 1)),
                            after=(WO.rdeps() + CTX[cc].rdeps() +
                                   (Po.wdeps() if cc == 0 else [])))
                        WO.read(mm)
                        CTX[cc].read(mm)
                    Po.wrote(mm)
                    ts = em.op("vector", lambda e, P=Po, o=oc, cl=col: e.scalar_tensor_tensor(
                        self.XR[o].ap[:, cl:cl + 512], P.ap[:], self.CP.ap[:, o:o + 1],
                        self.XR[o].ap[:, cl:cl + 512], ALU.add, ALU.add),
                        after=Po.rdeps() + self.XR[oc].wdeps() + self.CP.rdeps())
                    Po.read(ts)
                    self.XR[oc].wrote(ts)
        em.barrier()
        esO.close()
        esA.close()



    def topk_threshold(self, tag, srow_buf, es):
        em, nc = self.em, self.nc
        scr = nc.dram_tensor(f"scr_{tag}", [T], F32, kind="Internal").ap()
        S2D = self.sb(f"{tag}s2d", [128, 16], F32, es)
        t1 = em.dma("gpsimd", lambda e: e.dma_start(
            out=scr.rearrange("(a b) -> a b", a=1), in_=srow_buf.ap[:]),
            after=srow_buf.rdeps())
        srow_buf.read(t1)
        t2 = em.dma("gpsimd", lambda e: e.dma_start(
            out=S2D.ap[:], in_=scr.rearrange("(p f) -> p f", p=128)),
            after=[t1] + S2D.wdeps())
        S2D.wrote(t2)
        LO = self.sb(f"{tag}lo", [1, 1], F32, es)
        HI = self.sb(f"{tag}hi", [1, 1], F32, es)
        OF1 = self.sb(f"{tag}of1", [1, 128], F32, es)
        TROW = self.sb(f"{tag}trow", [1, 256], F32, es)
        TMIX = self.sb(f"{tag}tmix", [1, 16], F32, es)
        SPAN = self.sb(f"{tag}span", [1, 1], F32, es)
        C01 = self.sb(f"{tag}c01", [128, 256], F32, es)
        PCF = self.sb(f"{tag}pcf", [128, 16], F32, es)
        PC = self.sb(f"{tag}pc", [128, 16], BF16, es)
        CNT = self.sb(f"{tag}cnt", [1, 16], F32, es)
        G = self.sb(f"{tag}g", [1, 16], F32, es)
        LO.wrote(em.op("vector", lambda e: e.memset(LO.ap[:], -512.0)))
        HI.wrote(em.op("vector", lambda e: e.memset(HI.ap[:], 512.0)))
        OF1.wrote(em.op("vector", lambda e: e.memset(OF1.ap[:], 1.0)))
        Pt = Pc = self.P[5]
        srep = S2D.ap[:].rearrange("p (a b) -> p a b", a=1).to_broadcast([128, 16, 16])
        def emit_round(it):
            s1 = em.op("vector", lambda e: e.tensor_tensor(
                SPAN.ap[:], HI.ap[:], LO.ap[:], ALU.subtract),
                after=HI.rdeps() + LO.rdeps() + SPAN.wdeps())
            SPAN.wrote(s1)
            s2 = em.op("vector", lambda e: e.tensor_tensor(
                TROW.ap[:], self.BIS.ap[0:1, 0:256], SPAN.ap[:].to_broadcast([1, 256]),
                ALU.mult), after=[s1] + self.BIS.rdeps() + TROW.wdeps())
            SPAN.read(s2); TROW.wrote(s2)
            s3 = em.op("vector", lambda e: e.tensor_tensor(
                TROW.ap[:], TROW.ap[:], LO.ap[:].to_broadcast([1, 256]), ALU.add),
                after=[s2] + LO.rdeps())
            LO.read(s3); TROW.wrote(s3)
            b1 = em.op("tensor", lambda e: e.matmul(
                Pt.ap[:, 0:256], OF1.ap[:], TROW.ap[:], start=True, stop=True),
                after=[s3] + OF1.rdeps() + Pt.wdeps())
            TROW.read(b1); Pt.wrote(b1)
            b2 = em.op("vector", lambda e: e.tensor_tensor(
                C01.ap[:], srep, Pt.ap[:, 0:256].rearrange("p (a b) -> p a b", a=16),
                ALU.is_gt), after=S2D.rdeps() + Pt.rdeps() + C01.wdeps())
            S2D.read(b2); Pt.read(b2); C01.wrote(b2)
            b3 = em.op("vector", lambda e: e.tensor_reduce(
                PCF.ap[:], C01.ap[:].rearrange("p (a b) -> p a b", a=16),
                axis=AX.X, op=ALU.add), after=[b2] + PCF.wdeps())
            C01.read(b3); PCF.wrote(b3)
            b4 = em.op("vector", lambda e: e.tensor_copy(PC.ap[:], PCF.ap[:]),
                       after=[b3] + PC.wdeps())
            PCF.read(b4); PC.wrote(b4)
            b5 = em.op("tensor", lambda e: e.matmul(
                Pc.ap[0:1, 0:16], self.OC.ap[:], PC.ap[:], start=True, stop=True),
                after=[b4] + self.OC.rdeps() + Pc.wdeps())
            PC.read(b5); Pc.wrote(b5)
            b6 = em.op("scalar", lambda e: e.copy(CNT.ap[:], Pc.ap[0:1, 0:16]),
                       after=Pc.rdeps() + CNT.wdeps())
            Pc.read(b6); CNT.wrote(b6)
            b7 = em.op("vector", lambda e: e.tensor_scalar(
                G.ap[:], CNT.ap[:], 1023.5, None, ALU.is_gt),
                after=[b6] + G.wdeps())
            CNT.read(b7); G.wrote(b7)
            tauv = TROW.ap[0:1, :].rearrange("o (a b) -> o a b", b=16)[:, :, 0]
            c1 = em.op("vector", lambda e, tv=tauv: e.tensor_tensor(
                TMIX.ap[:], tv, LO.ap[:].to_broadcast([1, 16]), ALU.subtract),
                after=[s3, b7] + LO.rdeps() + TMIX.wdeps())
            TROW.read(c1); LO.read(c1); TMIX.wrote(c1)
            c2 = em.op("vector", lambda e: e.tensor_tensor(
                TMIX.ap[:], TMIX.ap[:], G.ap[:], ALU.mult), after=[c1] + G.rdeps())
            G.read(c2); TMIX.wrote(c2)
            c3 = em.op("vector", lambda e: e.tensor_tensor(
                TMIX.ap[:], TMIX.ap[:], LO.ap[:].to_broadcast([1, 16]), ALU.add),
                after=[c2] + LO.rdeps())
            LO.read(c3); TMIX.wrote(c3)
            c4 = em.op("vector", lambda e: e.tensor_reduce(
                LO.ap[:], TMIX.ap[:], axis=AX.X, op=ALU.max),
                after=[c3] + LO.wdeps())
            TMIX.read(c4); LO.wrote(c4)
            d1 = em.op("vector", lambda e, tv=tauv: e.tensor_tensor(
                TMIX.ap[:], tv, HI.ap[:].to_broadcast([1, 16]), ALU.subtract),
                after=[s3, c4] + HI.rdeps() + TMIX.wdeps())
            TROW.read(d1); HI.read(d1); TMIX.wrote(d1)
            d2 = em.op("vector", lambda e: e.tensor_scalar(
                G.ap[:], G.ap[:], -1.0, 1.0, ALU.mult, ALU.add), after=G.wdeps() + [c2])
            G.wrote(d2)
            d3 = em.op("vector", lambda e: e.tensor_tensor(
                TMIX.ap[:], TMIX.ap[:], G.ap[:], ALU.mult), after=[d1, d2])
            G.read(d3); TMIX.wrote(d3)
            d4 = em.op("vector", lambda e: e.tensor_tensor(
                TMIX.ap[:], TMIX.ap[:], HI.ap[:].to_broadcast([1, 16]), ALU.add),
                after=[d3] + HI.rdeps())
            HI.read(d4); TMIX.wrote(d4)
            d5 = em.op("vector", lambda e: e.tensor_reduce(
                HI.ap[:], TMIX.ap[:], axis=AX.X, op=ALU.min),
                after=[d4] + HI.wdeps())
            TMIX.read(d5); HI.wrote(d5)
        rounds = [lambda it=it: emit_round(it) for it in range(9)]
        return LO, HI, rounds

    def l0_ffn(self):
        em, nc = self.em, self.nc
        es = ExitStack()
        w1d = self.dram_in("W1g0", [NC, 128, 4 * 1024])
        w2d = self.dram_in("W2g0", [NC, 128, 4 * 1024])
        HB = self.sb("HB0", [128, FC * 512], BF16, es)
        W1B = [self.sb(f"W1B{i}", [128, 4096], BF16, es) for i in range(3)]
        W2B = [self.sb(f"W2B{i}", [128, 4096], BF16, es) for i in range(2)]
        SROW = self.sb("SROW0", [1, T], F32, es)
        GROW = self.sb("GROW0", [1, T], F32, es)
        GRB = self.sb("GRB0", [1, T], BF16, es)
        WRB = self.sb("WRB0", [128, 8], BF16, es)
        GB = self.sb("GB0", [128, 512], BF16, es)
        TMP = [self.sb(f"F0T{i}", [128, 512], F32, es) for i in range(2)]

        t = em.op("vector", lambda e: e.tensor_copy(WRB.ap[:], self.CP.ap[:, 48:56]),
                  after=self.CP.rdeps() + WRB.wdeps())
        WRB.wrote(t)
        Pr = self.P[4]
        for tt in range(NTT):
            col = tt * 512
            mm = None
            for kc in range(NC):
                mm = em.op("tensor", lambda e, k=kc, c=col: e.matmul(
                    Pr.ap[0:1, :], WRB.ap[:, k:k + 1], self.XB[k].ap[:, c:c + 512],
                    start=(k == 0), stop=(k == NC - 1)),
                    after=(WRB.rdeps() + self.XB[kc].rdeps() +
                           (Pr.wdeps() if kc == 0 else [])))
                WRB.read(mm); self.XB[kc].read(mm)
            Pr.wrote(mm)
            te = em.op("scalar", lambda e, c=col: e.copy(
                SROW.ap[0:1, c:c + 512], Pr.ap[0:1, :]),
                after=Pr.rdeps() + SROW.wdeps())
            Pr.read(te); SROW.wrote(te)
        LO, HI, rounds = self.topk_threshold("tk0", SROW, es)
        GBF = self.sb("GBF0", [128, T], BF16, es)

        def emit_gates():
            g1 = em.op("vector", lambda e: e.tensor_tensor(
                GROW.ap[:], SROW.ap[:], LO.ap[:].to_broadcast([1, T]), ALU.is_gt),
                after=SROW.rdeps() + LO.rdeps() + GROW.wdeps())
            SROW.read(g1); LO.read(g1); GROW.wrote(g1)
            g2 = em.op("scalar", lambda e: e.activation(
                GRB.ap[:], SROW.ap[:], AF.Sigmoid), after=SROW.rdeps() + GRB.wdeps())
            SROW.read(g2); GRB.wrote(g2)
            g3 = em.op("vector", lambda e: e.tensor_tensor(
                GRB.ap[:], GRB.ap[:], GROW.ap[:], ALU.mult),
                after=[g1, g2] + GRB.wdeps())
            GROW.read(g3); GRB.wrote(g3)
            for tt in range(NTT):
                col = tt * 512
                Pg = self.P[5]
                bg = em.op("tensor", lambda e, c=col: e.matmul(
                    Pg.ap[:], self.OB1.ap[:], GRB.ap[0:1, c:c + 512], start=True, stop=True),
                    after=GRB.rdeps() + self.OB1.rdeps() + Pg.wdeps())
                GRB.read(bg); Pg.wrote(bg)
                bg2 = em.op("vector", lambda e, c=col: e.tensor_copy(
                    GBF.ap[:, c:c + 512], Pg.ap[:]), after=Pg.rdeps() + GBF.wdeps())
                Pg.read(bg2); GBF.wrote(bg2)
        hooks0 = rounds + [emit_gates]

        def evac0(oc, tt, P, GBFb):
            col = tt * 512
            tmp = TMP[oc % 2]
            e1 = em.op("vector", lambda e, Pp=P, o=oc, t=tmp, c=col: e.scalar_tensor_tensor(
                t.ap[:], Pp.ap[:], self.CP.ap[:, 40 + o:41 + o], GBFb.ap[:, c:c + 512],
                ALU.add, ALU.mult),
                after=P.rdeps() + GBFb.rdeps() + tmp.wdeps() + self.CP.rdeps())
            P.read(e1); GBFb.read(e1); tmp.wrote(e1)
            e2 = em.op("vector", lambda e, o=oc, t=tmp, c=col: e.scalar_tensor_tensor(
                self.XR[o].ap[:, c:c + 512], self.XR[o].ap[:, c:c + 512], 2.0,
                t.ap[:], ALU.mult, ALU.add),
                after=[e1] + self.XR[oc].wdeps())
            tmp.read(e2); self.XR[oc].wrote(e2)

        self.ffn_core("f0", w1d, w2d, 8, 40, AF.Gelu_apprx_tanh, HB, W1B, W2B, GBF, evac0,
                      hooks=hooks0)
        em.barrier()
        es.close()




    def ffn_core(self, tag, w1d, w2d, b1_off, b2_off, act, HB, W1B, W2B, GBF, evac,
                 hooks=None):
        em = self.em
        HBc = [Buf(HB.ap[:, h * 512:(h + 1) * 512]) for h in range(FC)]
        for tt in range(NTT):
            col = tt * 512

            def emit_A(hc):
                hg, hq = hc // 4, hc % 4
                w1b = W1B[hg % 3]
                if hq == 0:
                    tw = em.dma("gpsimd", lambda e, g=hg, b=w1b: e.dma_start(
                        out=b.ap[:], in_=w1d[g]), after=w1b.wdeps())
                    w1b.wrote(tw)
                Ph = self.P[6 + hc % 2]
                mm = None
                for kc in range(NC):
                    mm = em.op("tensor", lambda e, k=kc, q=hq, P=Ph, b=w1b, c=col: e.matmul(
                        P.ap[:], b.ap[:, q * 1024 + k * 128: q * 1024 + k * 128 + 128],
                        self.XB[k].ap[:, c:c + 512],
                        start=(k == 0), stop=(k == NC - 1)),
                        after=(w1b.rdeps() + self.XB[kc].rdeps() +
                               (Ph.wdeps() if kc == 0 else [])))
                    w1b.read(mm); self.XB[kc].read(mm)
                Ph.wrote(mm)
                ev = em.op("scalar", lambda e, P=Ph, h=hc: e.activation(
                    HBc[h].ap, P.ap[:], act,
                    bias=self.CP.ap[:, b1_off + h:b1_off + h + 1], scale=1.0),
                    after=Ph.rdeps() + HBc[hc].wdeps() + self.CP.rdeps())
                Ph.read(ev); HBc[hc].wrote(ev)

            def emit_B1(hc):
                hg, hq = hc // 4, hc % 4
                w2b = W2B[hg % 2]
                if hq == 0:
                    tw = em.dma("gpsimd", lambda e, g=hg, b=w2b: e.dma_start(
                        out=b.ap[:], in_=w2d[g]), after=w2b.wdeps())
                    w2b.wrote(tw)
                for oc in range(5):
                    P = self.P[oc]
                    mm = em.op("tensor", lambda e, q=hq, o=oc, b=w2b, h=hc, Pp=P: e.matmul(
                        Pp.ap[:], b.ap[:, q * 1024 + o * 128: q * 1024 + o * 128 + 128],
                        HBc[h].ap, start=(h == 0), stop=(h == FC - 1)),
                        after=(w2b.rdeps() + HBc[hc].rdeps() +
                               (P.wdeps() if hc == 0 else [])))
                    w2b.read(mm); HBc[hc].read(mm)
                    if hc == FC - 1:
                        P.wrote(mm)

            def emit_B2(hc):
                hg, hq = hc // 4, hc % 4
                w2b = W2B[hg % 2]
                if hq == 0:
                    tw = em.dma("gpsimd", lambda e, g=hg, b=w2b: e.dma_start(
                        out=b.ap[:], in_=w2d[g]), after=w2b.wdeps())
                    w2b.wrote(tw)
                for oi, oc in enumerate((5, 6, 7)):
                    P = self.P[oi]
                    mm = em.op("tensor", lambda e, q=hq, o=oc, b=w2b, h=hc, Pp=P: e.matmul(
                        Pp.ap[:], b.ap[:, q * 1024 + o * 128: q * 1024 + o * 128 + 128],
                        HBc[h].ap, start=(h == 0), stop=(h == FC - 1)),
                        after=(w2b.rdeps() + HBc[hc].rdeps() +
                               (P.wdeps() if hc == 0 else [])))
                    w2b.read(mm); HBc[hc].read(mm)
                    if hc == FC - 1:
                        P.wrote(mm)

            emit_A(0)
            emit_A(1)
            for hc in range(FC):
                if hc + 2 < FC:
                    emit_A(hc + 2)
                if hooks and hc % 2 == 0:
                    hooks.pop(0)()
                emit_B1(hc)
            while hooks:
                hooks.pop(0)()
            for oc in (0, 1, 2):
                evac(oc, tt, self.P[oc], GBF)
            for hc in range(FC):
                emit_B2(hc)
            for oc in (3, 4):
                evac(oc, tt, self.P[oc], GBF)
            for oi, oc in enumerate((5, 6, 7)):
                evac(oc, tt, self.P[oi], GBF)


    def l1_attn(self, debug=False):
        em, nc = self.em, self.nc
        if debug:
            self.dbg_ek = nc.dram_tensor("dbg_ek", [D, T], F32, kind="ExternalOutput").ap()
            self.dbg_y = nc.dram_tensor("dbg_y", [D, T], F32, kind="ExternalOutput").ap()
            self.dbg_a = nc.dram_tensor("dbg_a", [D, T], F32, kind="ExternalOutput").ap()
        esL = ExitStack()
        ktd = self.dram_in("KTh", [128, NC * 1024])
        vtd = self.dram_in("VTh", [128, NC * 1024])
        rtd = self.dram_in("RTh", [128, NC * 1024])
        otd = self.dram_in("OTh", [128, NC * 1024])

        DCOL = self.sb("DCOL", [128, NC], F32, esL)
        EUCOL = self.sb("EUCOL", [128, NC], F32, esL)
        CAB = self.sb("CAB", [128, NC], F32, esL)
        CBB = self.sb("CBB", [128, NC], F32, esL)
        TCOL = self.sb("TCOL", [128, 1], F32, esL)
        t0 = em.op("vector", lambda e: e.memset(CAB.ap[:], 0.0), after=CAB.wdeps())
        CAB.wrote(t0)
        t0 = em.op("vector", lambda e: e.memset(CBB.ap[:], 0.0), after=CBB.wdeps())
        CBB.wrote(t0)
        for cc in range(NC):
            a1 = em.op("scalar", lambda e, c=cc: e.activation(
                TCOL.ap[:], self.CP.ap[:, 56 + c:57 + c], AF.Exp),
                after=self.CP.rdeps() + TCOL.wdeps())
            TCOL.wrote(a1)
            a2 = em.op("vector", lambda e: e.tensor_scalar(
                TCOL.ap[:], TCOL.ap[:], -1.0, None, ALU.mult), after=TCOL.wdeps())
            TCOL.wrote(a2)
            a3 = em.op("scalar", lambda e, c=cc: e.activation(
                DCOL.ap[:, c:c + 1], TCOL.ap[:], AF.Exp),
                after=[a2] + DCOL.wdeps())
            TCOL.read(a3); DCOL.wrote(a3)
            a4 = em.op("scalar", lambda e, c=cc: e.activation(
                EUCOL.ap[:, c:c + 1], self.CP.ap[:, 64 + c:65 + c], AF.Exp),
                after=self.CP.rdeps() + EUCOL.wdeps())
            EUCOL.wrote(a4)

        esH = ExitStack()
        NB_ = 4
        def alias(k, lo):
            b = Buf(self.XB[k].ap[:, lo * 512:(lo + 1) * 512])
            b.w = self.XB[k].w
            b.r = list(self.XB[k].r)
            return b
        XK = [[alias(k, 0 + i) for k in range(NC)] for i in range(2)]
        XV = [[alias(k, 2 + i) for k in range(NC)] for i in range(2)]
        XRr = [[self.sb(f"XRr{i}{k}", [128, 512], BF16, esH) for k in range(NC)]
               for i in range(2)]
        DD = [self.sb(f"DDl{i}", [128, 512], F32, esH) for i in range(2)]
        EKb = [self.sb(f"EKb{i}", [128, 512], F32, esH) for i in range(2)]
        VV = [self.sb(f"VVb{i}", [128, 512], F32, esH) for i in range(2)]
        EKV = [self.sb(f"EKVb{i}", [128, 512], F32, esH) for i in range(2)]
        Ab = [self.sb(f"Ab{i}", [128, 513], F32, esH) for i in range(2)]
        Bb = [self.sb(f"Bb{i}", [128, 513], F32, esH) for i in range(2)]
        NUM = [self.sb(f"NUMb{i}", [128, 512], F32, esH) for i in range(2)]
        SRB = [self.sb(f"SRBb{i}", [128, 512], BF16, esH) for i in range(2)]
        LT = [self.sb("LTl0", [128, 512], F32, esH)]
        RWK = [self.sb(f"RWK{i}", [128, NC * 512], BF16, esH) for i in range(2)]
        KTA = self.sb("KTA", [128, NC * 1024], BF16, esH)
        VTA = self.sb("VTA", [128, NC * 1024], BF16, esH)
        RTA = [self.sb(f"RTA{i}", [128, 1024], BF16, esH) for i in range(2)]
        OTB = self.sb("OTB", [128, NC * 512], BF16, esH)
        for (tb, dsrc) in ((KTA, ktd), (VTA, vtd)):
            tw = em.dma("gpsimd", lambda e, b=tb, s=dsrc: e.dma_start(
                out=b.ap[:], in_=s), after=tb.wdeps())
            tb.wrote(tw)
        rt_bf = nc.dram_tensor("rt_bf", [128, NC * 1024], BF16, kind="Internal").ap()
        ot_bf = nc.dram_tensor("ot_bf", [128, NC * 1024], BF16, kind="Internal").ap()
        RTC = Buf(rt_bf)
        OTC = Buf(ot_bf)
        RTC.wrote(em.dma("gpsimd", lambda e: e.dma_start(out=rt_bf, in_=rtd)))
        OTC.wrote(em.dma("gpsimd", lambda e: e.dma_start(out=ot_bf, in_=otd)))

        def emit_lerps(blk):
            pp = blk % 2
            h0 = blk * 512
            for kc in range(NC):
                dd = DD[kc % 2]
                if blk == 0:
                    d1 = em.op("vector", lambda e, k=kc, d=dd: e.tensor_tensor(
                        d.ap[:, 0:511], self.XR[k].ap[:, 1:512],
                        self.XR[k].ap[:, 0:511], ALU.subtract),
                        after=self.XR[kc].rdeps() + dd.wdeps())
                else:
                    d1 = em.op("vector", lambda e, k=kc, d=dd, hh=h0: e.tensor_tensor(
                        d.ap[:], self.XR[k].ap[:, hh:hh + 512],
                        self.XR[k].ap[:, hh - 1:hh + 511], ALU.subtract),
                        after=self.XR[kc].rdeps() + dd.wdeps())
                self.XR[kc].read(d1); dd.wrote(d1)
                for li, (dst, coff) in enumerate(((XK, 72), (XV, 80), (XRr, 88))):
                    buf = dst[pp][kc]
                    if li < 2:
                        if blk == 0:
                            s1 = em.op("vector", lambda e, b=buf, k=kc, co=coff, d=dd: e.scalar_tensor_tensor(
                                b.ap[:, 1:512], d.ap[:, 0:511],
                                self.CP.ap[:, co + k:co + k + 1],
                                self.XR[k].ap[:, 0:511], ALU.mult, ALU.add),
                                after=[d1] + buf.wdeps() + self.CP.rdeps())
                            buf.wrote(s1)
                            s2 = em.op("vector", lambda e, b=buf, k=kc, co=coff: e.tensor_scalar(
                                b.ap[:, 0:1], self.XR[k].ap[:, 0:1],
                                self.CP.ap[:, co + k:co + k + 1], None, ALU.mult),
                                after=buf.wdeps() + self.XR[kc].rdeps())
                            buf.wrote(s2)
                        else:
                            s1 = em.op("vector", lambda e, b=buf, k=kc, co=coff, d=dd, hh=h0: e.scalar_tensor_tensor(
                                b.ap[:], d.ap[:], self.CP.ap[:, co + k:co + k + 1],
                                self.XR[k].ap[:, hh - 1:hh + 511], ALU.mult, ALU.add),
                                after=[d1] + buf.wdeps() + self.CP.rdeps())
                            buf.wrote(s1)
                    else:
                        lt = LT[0]
                        if blk == 0:
                            p1 = em.op("gpsimd", lambda e, k=kc, co=coff, d=dd, t=lt: e.tensor_tensor(
                                t.ap[:, 0:511], d.ap[:, 0:511],
                                self.CP.ap[:, co + k:co + k + 1].to_broadcast([128, 511]),
                                ALU.mult), after=[d1] + lt.wdeps() + self.CP.rdeps())
                            dd.read(p1); lt.wrote(p1)
                            p2 = em.op("gpsimd", lambda e, b=buf, k=kc, t=lt: e.tensor_tensor(
                                b.ap[:, 1:512], t.ap[:, 0:511],
                                self.XR[k].ap[:, 0:511], ALU.add),
                                after=[p1] + buf.wdeps())
                            lt.read(p2); buf.wrote(p2)
                            p3 = em.op("gpsimd", lambda e, b=buf, k=kc, co=coff: e.tensor_tensor(
                                b.ap[:, 0:1], self.XR[k].ap[:, 0:1],
                                self.CP.ap[:, co + k:co + k + 1], ALU.mult),
                                after=buf.wdeps() + self.XR[kc].rdeps())
                            buf.wrote(p3)
                        else:
                            p1 = em.op("gpsimd", lambda e, k=kc, co=coff, d=dd, t=lt: e.tensor_tensor(
                                t.ap[:], d.ap[:],
                                self.CP.ap[:, co + k:co + k + 1].to_broadcast([128, 512]),
                                ALU.mult), after=[d1] + lt.wdeps() + self.CP.rdeps())
                            dd.read(p1); lt.wrote(p1)
                            p2 = em.op("gpsimd", lambda e, b=buf, k=kc, t=lt, hh=h0: e.tensor_tensor(
                                b.ap[:], t.ap[:],
                                self.XR[k].ap[:, hh - 1:hh + 511], ALU.add),
                                after=[p1] + buf.wdeps())
                            lt.read(p2); buf.wrote(p2)
                    dd.read(buf.w)
                    self.XR[kc].read(buf.w)

        def emit_proj(blk, cc):
            pp = blk % 2
            sp = cc % 2
            ekb, vv, ekv_b, srb = EKb[sp], VV[sp], EKV[sp], SRB[sp]
            rta = RTA[(blk * NC + cc) % 2]
            tr = em.dma("sync", lambda e, b=rta, c=cc: e.dma_start(
                out=b.ap[:], in_=rt_bf[:, c * 1024:(c + 1) * 1024]),
                after=rta.wdeps() + RTC.rdeps())
            RTC.read(tr)
            rta.wrote(tr)
            for pi, (wb, xin, outbuf, actf) in enumerate((
                    (KTA, XK[pp], ekb, AF.Exp), (VTA, XV[pp], vv, AF.Identity),
                    (rta, XRr[pp], srb, AF.Sigmoid))):
                Pp = self.P[pi + 3 * sp]
                mm = None
                for kc in range(NC):
                    off = (cc * 1024 if wb in (KTA, VTA) else 0)
                    mm = em.op("tensor", lambda e, k=kc, w=wb, P=Pp, xi=xin, o=off: e.matmul(
                        P.ap[:], w.ap[:, o + k * 128: o + k * 128 + 128],
                        xi[k].ap[:], start=(k == 0), stop=(k == NC - 1)),
                        after=(wb.rdeps() + xin[kc].rdeps() +
                               (Pp.wdeps() if kc == 0 else [])))
                    wb.read(mm); xin[kc].read(mm)
                Pp.wrote(mm)
                ev = em.op("scalar", lambda e, P=Pp, o=outbuf, f=actf: e.activation(
                    o.ap[:], P.ap[:], f),
                    after=Pp.rdeps() + outbuf.wdeps())
                Pp.read(ev); outbuf.wrote(ev)
            ekv = em.op("gpsimd", lambda e, a=ekv_b, x_=ekb, y_=vv: e.tensor_tensor(
                a.ap[:], x_.ap[:], y_.ap[:], ALU.mult),
                after=ekb.rdeps() + vv.rdeps() + ekv_b.wdeps())
            ekb.read(ekv); vv.read(ekv); ekv_b.wrote(ekv)

        def emit_scan(blk, cc):
            pp = blk % 2
            sp = cc % 2
            ekb, vv, ekv_b = EKb[sp], VV[sp], EKV[sp]
            ab, bb, num, srb = Ab[sp], Bb[sp], NUM[sp], SRB[sp]
            rwk = RWK[blk % 2]
            ca = em.op("vector", lambda e, c=cc, a=ab: e.tensor_copy(
                a.ap[:, 0:1], CAB.ap[:, c:c + 1]),
                after=CAB.rdeps() + ab.wdeps())
            CAB.read(ca); ab.wrote(ca)
            sa = em.op("vector", lambda e, c=cc, a=ab, x_=ekv_b: e.tensor_tensor_scan(
                a.ap[:, 1:513], DCOL.ap[:, c:c + 1].to_broadcast([128, 512]),
                x_.ap[:], CAB.ap[:, c:c + 1], ALU.mult, ALU.add),
                after=[ca] + ekv_b.rdeps() + DCOL.rdeps() + CAB.rdeps() + ab.wdeps())
            DCOL.read(sa); CAB.read(sa); ekv_b.read(sa); ab.wrote(sa)
            ua = em.op("vector", lambda e, c=cc, a=ab: e.tensor_copy(
                CAB.ap[:, c:c + 1], a.ap[:, 512:513]),
                after=[sa] + CAB.wdeps())
            ab.read(ua); CAB.wrote(ua)
            cb = em.op("vector", lambda e, c=cc, a=bb: e.tensor_copy(
                a.ap[:, 0:1], CBB.ap[:, c:c + 1]),
                after=CBB.rdeps() + bb.wdeps())
            CBB.read(cb); bb.wrote(cb)
            sb_ = em.op("vector", lambda e, c=cc, a=bb, x_=ekb: e.tensor_tensor_scan(
                a.ap[:, 1:513], DCOL.ap[:, c:c + 1].to_broadcast([128, 512]),
                x_.ap[:], CBB.ap[:, c:c + 1], ALU.mult, ALU.add),
                after=[cb] + ekb.rdeps() + DCOL.rdeps() + CBB.rdeps() + bb.wdeps())
            DCOL.read(sb_); CBB.read(sb_); ekb.read(sb_); bb.wrote(sb_)
            ub = em.op("vector", lambda e, c=cc, a=bb: e.tensor_copy(
                CBB.ap[:, c:c + 1], a.ap[:, 512:513]),
                after=[sb_] + CBB.wdeps())
            bb.read(ub); CBB.wrote(ub)
            n1 = em.op("vector", lambda e, c=cc, a=ab, x_=ekv_b, n=num: e.scalar_tensor_tensor(
                n.ap[:], x_.ap[:], EUCOL.ap[:, c:c + 1], a.ap[:, 0:512],
                ALU.mult, ALU.add),
                after=ekv_b.rdeps() + [sa] + EUCOL.rdeps() + num.wdeps())
            ekv_b.read(n1); ab.read(n1); EUCOL.read(n1); num.wrote(n1)
            n2 = em.op("vector", lambda e, c=cc, a=bb, x_=ekb, o=ekv_b: e.scalar_tensor_tensor(
                o.ap[:], x_.ap[:], EUCOL.ap[:, c:c + 1], a.ap[:, 0:512],
                ALU.mult, ALU.add),
                after=[sb_, n1] + ekb.rdeps() + EUCOL.rdeps() + ekv_b.wdeps())
            ekb.read(n2); bb.read(n2); EUCOL.read(n2); ekv_b.wrote(n2)
            r1 = em.op("scalar", lambda e, x_=ekv_b, o=ekb: e.activation(
                o.ap[:], x_.ap[:], AF.Ln), after=[n2] + ekb.wdeps())
            ekv_b.read(r1); ekb.wrote(r1)
            r2 = em.op("scalar", lambda e, o=ekb: e.activation(
                o.ap[:], o.ap[:], AF.Exp, scale=-1.0), after=ekb.wdeps())
            ekb.wrote(r2)
            y1 = em.op("gpsimd", lambda e, n=num, o=ekb: e.tensor_tensor(
                n.ap[:], n.ap[:], o.ap[:], ALU.mult),
                after=[r2] + num.wdeps())
            ekb.read(y1); num.wrote(y1)
            y2 = em.op("gpsimd", lambda e, c=cc, n=num, s=srb, r=rwk: e.tensor_tensor(
                r.ap[:, c * 512:(c + 1) * 512], n.ap[:], s.ap[:], ALU.mult),
                after=[y1] + srb.rdeps() + rwk.wdeps())
            num.read(y2); srb.read(y2); rwk.wrote(y2)

        def emit_outproj(blk):
            h0 = blk * 512
            rwk = RWK[blk % 2]
            for oh in range(2):
                tw = em.dma("sync", lambda e, o=oh: e.dma_start(
                    out=OTB.ap[:].rearrange("p (c m) -> p c m", m=512),
                    in_=ot_bf.rearrange("p (c g m) -> p c g m", g=2, m=512)[:, :, o]),
                    after=OTB.wdeps() + OTC.rdeps())
                OTC.read(tw)
                OTB.wrote(tw)
                for oi in range(4):
                    oc = oh * 4 + oi
                    Po = self.P[6 + oi % 2]
                    mm = None
                    for cc in range(NC):
                        mm = em.op("tensor", lambda e, c=cc, o=oi, Pp=Po, r=rwk: e.matmul(
                            Pp.ap[:], OTB.ap[:, c * 512 + o * 128: c * 512 + o * 128 + 128],
                            r.ap[:, c * 512:(c + 1) * 512],
                            start=(c == 0), stop=(c == NC - 1)),
                            after=(OTB.rdeps() + rwk.rdeps() +
                                   (Po.wdeps() if cc == 0 else [])))
                        OTB.read(mm); rwk.read(mm)
                    Po.wrote(mm)
                    ts = em.op("vector", lambda e, Pp=Po, o=oc, hh=h0: e.tensor_tensor(
                        self.XR[o].ap[:, hh:hh + 512], self.XR[o].ap[:, hh:hh + 512],
                        Pp.ap[:], ALU.add),
                        after=Po.rdeps() + self.XR[oc].wdeps())
                    Po.read(ts); self.XR[oc].wrote(ts)

        emit_lerps(0)
        for blk in range(NB_):
            if blk + 1 < NB_:
                emit_lerps(blk + 1)
            emit_proj(blk, 0)
            for cc in range(1, NC):
                emit_proj(blk, cc)
                if cc == 4 and blk > 0:
                    emit_outproj(blk - 1)
                emit_scan(blk, cc - 1)
            emit_scan(blk, NC - 1)
        emit_outproj(NB_ - 1)
        em.barrier()
        esH.close()
        esL.close()




    def l1_ffn(self):
        em, nc = self.em, self.nc
        es = ExitStack()
        w1d = self.dram_in("W1g1", [NC, 128, 4 * 1024])
        w2d = self.dram_in("W2g1", [NC, 128, 4 * 1024])
        HB = self.sb("HB1", [128, FC * 512], BF16, es)
        W1B = [self.sb(f"X1B{i}", [128, 4096], BF16, es) for i in range(3)]
        W2B = [self.sb(f"X2B{i}", [128, 4096], BF16, es) for i in range(2)]
        SROW = self.sb("SROW1", [1, T], F32, es)
        GRB = self.sb("GRB1", [1, T], BF16, es)
        WRB = self.sb("WRB1", [128, 8], BF16, es)
        GB = self.sb("GB1", [128, 512], BF16, es)
        TMP = [self.sb(f"F1T{i}", [128, 512], F32, es) for i in range(2)]

        t = em.op("vector", lambda e: e.tensor_copy(WRB.ap[:], self.CP.ap[:, 136:144]),
                  after=self.CP.rdeps() + WRB.wdeps())
        WRB.wrote(t)
        Pr = self.P[4]
        for tt in range(NTT):
            col = tt * 512
            mm = None
            for kc in range(NC):
                mm = em.op("tensor", lambda e, k=kc, c=col: e.matmul(
                    Pr.ap[0:1, :], WRB.ap[:, k:k + 1], self.XB[k].ap[:, c:c + 512],
                    start=(k == 0), stop=(k == NC - 1)),
                    after=(WRB.rdeps() + self.XB[kc].rdeps() +
                           (Pr.wdeps() if kc == 0 else [])))
                WRB.read(mm); self.XB[kc].read(mm)
            Pr.wrote(mm)
            te = em.op("scalar", lambda e, c=col: e.activation(
                SROW.ap[0:1, c:c + 512], Pr.ap[0:1, :], AF.Identity,
                bias=self.CP.ap[0:1, 144:145], scale=1.0),
                after=Pr.rdeps() + SROW.wdeps() + self.CP.rdeps())
            Pr.read(te); SROW.wrote(te)
        LO, HI, rounds = self.topk_threshold("tk1", SROW, es)
        GBF = self.sb("GBF1", [128, T], BF16, es)

        def emit_gates():
            g1 = em.op("vector", lambda e: e.tensor_tensor(
                GRB.ap[:], SROW.ap[:], HI.ap[:].to_broadcast([1, T]), ALU.is_gt),
                after=SROW.rdeps() + HI.rdeps() + GRB.wdeps())
            SROW.read(g1); HI.read(g1); GRB.wrote(g1)
            for tt in range(NTT):
                col = tt * 512
                Pg = self.P[5]
                bg = em.op("tensor", lambda e, c=col: e.matmul(
                    Pg.ap[:], self.OB1.ap[:], GRB.ap[0:1, c:c + 512], start=True, stop=True),
                    after=GRB.rdeps() + self.OB1.rdeps() + Pg.wdeps())
                GRB.read(bg); Pg.wrote(bg)
                bg2 = em.op("vector", lambda e, c=col: e.tensor_copy(
                    GBF.ap[:, c:c + 512], Pg.ap[:]), after=Pg.rdeps() + GBF.wdeps())
                Pg.read(bg2); GBF.wrote(bg2)
        hooks1 = rounds + [emit_gates]

        def evac1(oc, tt, P, GBFb):
            col = tt * 512
            tmp = TMP[oc % 2]
            e1 = em.op("vector", lambda e, Pp=P, o=oc, t=tmp, c=col: e.scalar_tensor_tensor(
                t.ap[:], Pp.ap[:], self.CP.ap[:, 128 + o:129 + o],
                self.XR[o].ap[:, c:c + 512], ALU.add, ALU.subtract),
                after=P.rdeps() + self.XR[oc].wdeps() + tmp.wdeps() + self.CP.rdeps())
            P.read(e1); self.XR[oc].read(e1); tmp.wrote(e1)
            e2 = em.op("vector", lambda e, t=tmp, c=col: e.tensor_tensor(
                t.ap[:], t.ap[:], GBFb.ap[:, c:c + 512], ALU.mult),
                after=[e1] + GBFb.rdeps() + tmp.wdeps())
            GBFb.read(e2); tmp.wrote(e2)
            e3 = em.op("vector", lambda e, o=oc, t=tmp, c=col: e.scalar_tensor_tensor(
                self.XR[o].ap[:, c:c + 512], self.XR[o].ap[:, c:c + 512], 2.0,
                t.ap[:], ALU.mult, ALU.add),
                after=[e2] + self.XR[oc].wdeps())
            tmp.read(e3); self.XR[oc].wrote(e3)

        self.ffn_core("f1", w1d, w2d, 96, 128, AF.Gelu, HB, W1B, W2B, GBF, evac1,
                      hooks=hooks1)
        em.barrier()
        es.close()



    def finish(self):
        with self.nc.Block() as block:
            self.em.play(block)
        self.es.close()
        return self.nc



def _tile_wcol(W, ncol):
    K, M = W.shape
    return np.ascontiguousarray(
        W.reshape(K // 128, 128, M // ncol, ncol).transpose(2, 1, 0, 3))


def _tile_wrow(W):
    K, M = W.shape
    return np.ascontiguousarray(W.reshape(K // 128, 128, M).transpose(1, 0, 2))


def host_prep(inputs):
    slopes = _slopes16()
    d = {}
    d["Wqh"] = _tile_wcol(np.asarray(inputs["L0_attn_Wq"]), 128).reshape(NC, 128, 1024)
    d["Wkh"] = _tile_wcol(np.asarray(inputs["L0_attn_Wk"]), 128).reshape(NC, 128, 1024)
    Wv = np.asarray(inputs["L0_attn_Wv"])
    d["Wvh"] = np.ascontiguousarray(
        Wv.reshape(NC, 128, 2, 512).transpose(2, 1, 0, 3)).reshape(2, 128, NC * 512)
    Wo = np.asarray(inputs["L0_attn_Wo"])
    d["Woh"] = np.ascontiguousarray(
        Wo.reshape(NC, 128, 2, 512).transpose(2, 1, 0, 3)).reshape(2, 128, NC * 512)

    cp = np.zeros((128, 160), np.float32)
    def packcol(vec, c0):
        v = np.asarray(vec).reshape(-1, 128).T
        cp[:, c0:c0 + v.shape[1]] = v
        return c0 + v.shape[1]
    packcol(inputs["L0_attn_bo"], 0)
    packcol(inputs["L0_ffn_b1"], 8)
    packcol(inputs["L0_ffn_b2"], 40)
    packcol(inputs["L0_ffn_w_router"], 48)
    packcol(inputs["L1_attn_time_decay"], 56)
    packcol(inputs["L1_attn_time_first"], 64)
    packcol(inputs["L1_attn_time_mix_k"], 72)
    packcol(inputs["L1_attn_time_mix_v"], 80)
    packcol(inputs["L1_attn_time_mix_r"], 88)
    packcol(inputs["L1_ffn_b1"], 96)
    packcol(inputs["L1_ffn_b2"], 128)
    packcol(inputs["L1_ffn_w_r"].reshape(-1), 136)
    cp[0, 144] = float(np.asarray(inputs["L1_ffn_b_r"]).reshape(-1)[0])
    d["cpack"] = cp

    def grp4(W):
        K, M = W.shape
        return np.ascontiguousarray(
            W.reshape(K // 128, 128, M // 512, 4, 128)
             .transpose(2, 1, 3, 0, 4)
             .reshape(M // 512, 128, 4 * K))

    def grp4row(W):
        K, M = W.shape
        return np.ascontiguousarray(
            W.reshape(K // 512, 4, 128, M)
             .transpose(0, 2, 1, 3)
             .reshape(K // 512, 128, 4 * M))

    d["W1g0"] = grp4(np.asarray(inputs["L0_ffn_W1"]))
    d["W2g0"] = grp4row(np.asarray(inputs["L0_ffn_W2"]))

    def colblocks(Wt):
        return np.ascontiguousarray(
            _tile_wcol(Wt, 128).reshape(NC, 128, 1024)
            .transpose(1, 0, 2)).reshape(128, NC * 1024)

    d["KTh"] = colblocks(np.asarray(inputs["L1_attn_Wk"]).T)
    d["VTh"] = colblocks(np.asarray(inputs["L1_attn_Wv"]).T)
    d["RTh"] = colblocks(np.asarray(inputs["L1_attn_Wr"]).T)
    d["OTh"] = _tile_wrow(np.asarray(inputs["L1_attn_Wo"]).T).reshape(128, NC * 1024)
    d["W1g1"] = grp4(np.asarray(inputs["L1_ffn_W1"]))
    d["W2g1"] = grp4row(np.asarray(inputs["L1_ffn_W2"]))

    qloc = np.tile(np.arange(512, dtype=np.float64), NTT)
    kloc = np.tile(np.arange(128, dtype=np.float64), 16)
    augq = np.zeros((H, 4, T), np.float64)
    augk = np.zeros((H, 4, T), np.float64)
    biasp = np.zeros((128, H * 19), np.float32)
    for h in range(H):
        s = slopes[h]
        vq = -s * qloc
        vqh = vq.astype(BF).astype(np.float64)
        augq[h, 0] = 1.0
        augq[h, 1] = 1.0
        augq[h, 2] = vqh
        augq[h, 3] = vq - vqh
        vk = s * kloc
        vkh = vk.astype(BF).astype(np.float64)
        augk[h, 0] = vkh
        augk[h, 1] = vk - vkh
        augk[h, 2] = 1.0
        augk[h, 3] = 1.0
        for dv in range(-3, 16):
            biasp[:, h * 19 + dv + 3] = -s * (128.0 * dv) / 8.0
    d["augq"] = augq.astype(BF)
    d["augk"] = augk.astype(BF)
    d["biasp"] = biasp
    d["bisrow"] = ((np.arange(256) // 16 + 1) / 17.0).astype(np.float32).reshape(1, 256)

    m01 = np.zeros((128, 4, 512), np.float32)
    p = np.arange(128)[:, None]
    f = np.arange(512)[None, :]
    for j in range(4):
        m01[:, j, :] = (p <= f - 128 * j)
    d["m01"] = m01.reshape(128, 4 * 512).astype(BF)
    return d


_PROG_CACHE = {}


def _get_prog(blocks):
    key = tuple(blocks)
    if key not in _PROG_CACHE:
        b = Build(blocks)
        b.setup_consts()
        b.load_stream("xT")
        if "a0" in blocks:
            b.l0_attn()
            b.layernorm("ln0")
        if "f0" in blocks:
            b.l0_ffn()
            b.layernorm("ln1")
        if "a1" in blocks:
            b.l1_attn(debug=("dbg" in blocks))
            b.layernorm("ln2")
        if "f1" in blocks:
            b.l1_ffn()
            b.layernorm("ln3", last=("a0" in blocks))
        b.store_stream("yT")
        nc = b.finish()
        _PROG_CACHE[key] = (nc, sorted(b.need))
    return _PROG_CACHE[key]


def kernel(**inputs):
    x = np.asarray(inputs["L0_attn_x"])
    B = x.shape[0]
    hp = host_prep(inputs)

    nc, need = _get_prog(["a0", "f0", "a1", "f1"])
    in_maps = []
    for b in range(B):
        m = {k: hp[k] for k in need if k in hp}
        m["xT"] = np.ascontiguousarray(x[b].T)
        in_maps.append(m)
    res = run_bass_kernel_spmd(nc, in_maps, core_ids=list(range(B)))
    out = np.stack([np.asarray(r["yT"]).T for r in res.results])
    return out.astype(np.float32)
